# version 03
# baseline (speedup 1.0000x reference)
import numpy as np
from contextlib import ExitStack

import concourse.bass as bass
import concourse.mybir as mybir
from concourse.bass_utils import run_bass_kernel_spmd
from concourse.alu_op_type import AluOpType as ALU
import ml_dtypes

F32 = mybir.dt.float32
BF16 = mybir.dt.bfloat16
I32 = mybir.dt.int32
AF = mybir.ActivationFunctionType
AX = mybir.AxisListType

NEG = -1.0e30
B, T, D = 4, 2048, 2048
NH, HD = 16, 128
TOK = 1024


class Prog:
    def __init__(self, nc):
        self.nc = nc
        self.items = []
        self.cnt = {s: 0 for s in ("pe", "ve", "se", "ge", "dcc")}
        self.waited = {}
        self.w = {}
        self.r = {}
        self.sems = {}
        self.floor = {}
        self._es = None

    def setup(self, es):
        self._es = es
        for s in self.cnt:
            self.sems[s] = es.enter_context(self.nc.semaphore("S" + s))

    def _sem(self, name):
        if name not in self.sems:
            self.sems[name] = self._es.enter_context(
                self.nc.semaphore("S" + name.replace(":", "_")))
            self.cnt[name] = 0
        return self.sems[name]

    def _wait(self, ek, sem, val):
        if val <= 0 or (ek == "pe" and sem == "pe"):
            return
        if self.waited.get((ek, sem), 0) >= val:
            return
        self.waited[(ek, sem)] = val
        s = self.sems[sem]
        self.items.append((ek, lambda e, s=s, val=val: e.wait_ge(s, val)))

    def op(self, ek, fn, reads=(), writes=(), sem=None, inc=1):
        if ek in self.floor:
            for sm, v in self.floor.pop(ek).items():
                self._wait(ek, sm, v)
        sem = sem or ek
        for b in reads:
            if b in self.w:
                self._wait(ek, *self.w[b])
        for b in writes:
            if b in self.w:
                self._wait(ek, *self.w[b])
            for t in self.r.get(b, ()):
                self._wait(ek, *t)
        self.cnt[sem] += inc
        tok = (sem, self.cnt[sem])
        s = self.sems[sem]
        self.items.append((ek, lambda e, fn=fn, s=s, inc=inc:
                           fn(e, lambda inst: inst.then_inc(s, inc))))
        for b in reads:
            self.r.setdefault(b, []).append(tok)
        for b in writes:
            self.w[b] = tok
            self.r[b] = []
        return tok

    def dma(self, queue, out_ap, in_ap, reads=(), writes=()):
        ek = "sp" if queue == "sp" else "ge"
        key = "d:" + (writes[0] if writes else "anon") + (":g" if ek == "ge" else "")
        self._sem(key)
        return self.op(ek, lambda e, cb: cb(e.dma_start(out=out_ap, in_=in_ap)),
                       reads=reads, writes=writes, sem=key, inc=16)

    def barrier(self):
        snap = {s: v for s, v in self.cnt.items() if v > 0}
        for ek in ("pe", "ve", "se", "ge", "sp"):
            self.floor[ek] = dict(snap)
        self.w = {}
        self.r = {}

    def tail(self):
        for sem, v in self.cnt.items():
            if (sem.startswith("d:") or sem == "dcc") and v > 0:
                s = self.sems[sem]
                self.items.append(("sp", lambda e, s=s, v=v: e.wait_ge(s, v)))

    def finalize(self, block):
        st = {"pe": [], "ve": [], "se": [], "ge": [], "sp": []}
        for ek, fn in self.items:
            st[ek].append(fn)

        @block.tensor
        def _(t):
            for fn in st["pe"]:
                fn(t)

        @block.vector
        def _(v):
            for fn in st["ve"]:
                fn(v)

        @block.scalar
        def _(s):
            for fn in st["se"]:
                fn(s)

        @block.gpsimd
        def _(g):
            for fn in st["ge"]:
                fn(g)

        @block.sync
        def _(sy):
            for fn in st["sp"]:
                fn(sy)


def mms(p, mmlist, reads=(), writes=()):
    mmlist = list(mmlist)

    def fn(t, cb):
        inst = None
        for (o, l, r, st, sp) in mmlist:
            inst = t.matmul(o, l, r, start=st, stop=sp)
        cb(inst)
    p.op("pe", fn, reads=reads, writes=writes)


def transp(p, out, in_, ident, reads=(), writes=()):
    p.op("pe", lambda t, cb: cb(t.transpose(out, in_, ident)),
         reads=reads, writes=writes)


def act(p, out, in_, func, reads=(), writes=(), bias=None, scale=None, accum=None):
    kw = {}
    if bias is not None:
        kw["bias"] = bias
    if scale is not None:
        kw["scale"] = scale
    if accum is not None:
        kw["accum_out"] = accum
    p.op("se", lambda s, cb: cb(s.activation(out, in_, func, **kw)),
         reads=reads, writes=writes)


def scopy(p, out, in_, reads=(), writes=(), bias=None, scale=None):
    f = AF.Copy if bias is None else AF.Identity
    act(p, out, in_, f, reads=reads, writes=writes, bias=bias, scale=scale)


def vtt(p, out, a, b, alu, reads=(), writes=(), eng="ve"):
    p.op(eng, lambda v, cb: cb(v.tensor_tensor(out, a, b, alu)),
         reads=reads, writes=writes)


def vts(p, out, in_, s1, op0, s2=None, op1=None, reads=(), writes=(), eng="ve"):
    if op1 is None:
        p.op(eng, lambda v, cb: cb(v.tensor_scalar(out, in_, s1, None, op0)),
             reads=reads, writes=writes)
    else:
        p.op(eng, lambda v, cb: cb(v.tensor_scalar(out, in_, s1, s2, op0, op1)),
             reads=reads, writes=writes)


def vstt(p, out, in0, scalar, in1, op0, op1, reads=(), writes=(), eng="ve"):
    p.op(eng, lambda v, cb: cb(v.scalar_tensor_tensor(out, in0, scalar, in1, op0, op1)),
         reads=reads, writes=writes)


def vcopy(p, out, in_, reads=(), writes=(), eng="ve"):
    p.op(eng, lambda v, cb: cb(v.tensor_copy(out, in_)), reads=reads, writes=writes)


def vmemset(p, ap_, val, reads=(), writes=(), eng="ve"):
    p.op(eng, lambda v, cb: cb(v.memset(ap_, val)), reads=reads, writes=writes)


def vreduce(p, out, in_, alu, reads=(), writes=(), absolute=False, eng="ve"):
    kw = {"apply_absolute_value": True} if absolute else {}
    p.op(eng, lambda v, cb: cb(v.tensor_reduce(out, in_, AX.X, alu, **kw)),
         reads=reads, writes=writes)


def vrecip(p, out, in_, reads=(), writes=()):
    p.op("ve", lambda v, cb: cb(v.reciprocal(out, in_)), reads=reads, writes=writes)


def build_l0a():
    nc = bass.Bass("TRN2", target_bir_lowering=False, debug=False)
    io = {}
    for n, sh, dt, kind in [
        ("xroll", [D, T], F32, "ExternalInput"),
        ("w_q", [D, D], F32, "ExternalInput"),
        ("w_k", [D, D], F32, "ExternalInput"),
        ("w_v", [D, D], F32, "ExternalInput"),
        ("w_o", [D, D], F32, "ExternalInput"),
        ("b_q_col", [128, 16], F32, "ExternalInput"),
        ("b_k_col", [128, 16], F32, "ExternalInput"),
        ("b_o_col", [128, 16], F32, "ExternalInput"),
        ("b_v_row", [1, D], F32, "ExternalInput"),
        ("triu0", [128, 128], F32, "ExternalInput"),
        ("onescol", [128, 1], F32, "ExternalInput"),
        ("onesrow", [1, 128], F32, "ExternalInput"),
        ("tmcol", [128, 16], F32, "ExternalInput"),
        ("epscol", [128, 1], F32, "ExternalInput"),
        ("lsel", [128, 1024], BF16, "ExternalInput"),
        ("oh8", [8, 1024], F32, "ExternalInput"),
        ("xnbuf", [D, T], BF16, "ExternalOutput"),
        ("xnf", [D, TOK], F32, "ExternalOutput"),
        ("p1", [D, TOK], F32, "ExternalOutput"),
    ]:
        io[n] = nc.dram_tensor(n, sh, dt, kind=kind).ap()

    p = Prog(nc)
    with ExitStack() as es:
        p.setup(es)
        emit_l0_attn(p, es, io)
        p.tail()
        block = es.enter_context(nc.Block())
        p.finalize(block)
    return nc


def emit_l0_attn(p, es, io):
    nc = p.nc
    cst = es.enter_context(nc.sbuf_tensor("cst", [128, 276], F32))
    cstb = es.enter_context(nc.sbuf_tensor("cstb", [128, 132], BF16))
    p.dma("sp", cst[:, 0:128], io["triu0"], writes=("cst",))
    p.dma("sp", cst[:, 128:129], io["onescol"], writes=("cst",))
    p.dma("sp", cst[0:1, 129:257], io["onesrow"], writes=("cst",))
    p.dma("sp", cst[:, 260:276], io["tmcol"], writes=("cst",))
    p.dma("sp", cst[:, 257:258], io["epscol"], writes=("cst",))
    p.dma("gd", cstb[:, 0:1], io["onescol"], writes=("cstb",))
    p.dma("gd", cstb[0:1, 1:129], io["onesrow"], writes=("cstb",))
    TRIU0 = cst[:, 0:128]
    ONESCOL_F = cst[:, 128:129]
    ONESROW_F = cst[0:1, 129:257]
    TMCOL = cst[:, 260:276]
    ONESCOL_B = cstb[:, 0:1]
    ONESROW_B = cstb[0:1, 1:129]

    biasq = es.enter_context(nc.sbuf_tensor("biasq", [128, 48], F32))
    p.dma("sp", biasq[:, 0:16], io["b_q_col"], writes=("biasq",))
    p.dma("sp", biasq[:, 16:32], io["b_k_col"], writes=("biasq",))
    p.dma("sp", biasq[:, 32:48], io["b_o_col"], writes=("biasq",))
    bvrow = es.enter_context(nc.sbuf_tensor("bvrow", [1, D], BF16))
    p.dma("gd", bvrow[:], io["b_v_row"], writes=("bvrow",))
    lsel = es.enter_context(nc.sbuf_tensor("s_lsel", [128, 1024], BF16))
    p.dma("sp", lsel[:], io["lsel"], writes=("lsel",))
    oh8 = es.enter_context(nc.sbuf_tensor("s_oh8", [8, 1024], F32))
    p.dma("sp", oh8[:], io["oh8"], writes=("oh8",))

    KT = es.enter_context(nc.sbuf_tensor("KT", [128, 8 * 2048], BF16))
    VB = es.enter_context(nc.sbuf_tensor("VB", [128, 16 * 1024], BF16))
    QT = es.enter_context(nc.sbuf_tensor("QT", [128, 8 * 1024], BF16))
    ATT = es.enter_context(nc.sbuf_tensor("ATT", [128, 16 * 1024], BF16))
    ps = [es.enter_context(nc.psum_tensor(f"ps{i}", [128, 512], F32)) for i in range(6)]
    psl128 = es.enter_context(nc.psum_tensor("psl", [128, 512], F32))
    ps_l = psl128[0:1, :]
    ps_b = es.enter_context(nc.psum_tensor("psb", [128, 512], F32))

    xr = io["xroll"].rearrange("(a p) t -> p a t", p=128)
    xnr = io["xnbuf"].rearrange("(a p) t -> p a t", p=128)

    with ExitStack() as ph:
        xc = ph.enter_context(nc.sbuf_tensor("xc", [128, 16 * 512], F32))
        sq = ph.enter_context(nc.sbuf_tensor("sq", [128, 16 * 512], BF16))
        xnb = ph.enter_context(nc.sbuf_tensor("xnb", [128, 16 * 512], BF16))
        xnf32 = ph.enter_context(nc.sbuf_tensor("xnf32", [128, 512], F32))
        srow = ph.enter_context(nc.sbuf_tensor("srow", [1, 512], F32))
        for c in range(4):
            p.dma("sp", xc.ap().rearrange("p (a b) -> p a b", b=512),
                  xr[:, :, c * 512:(c + 1) * 512], reads=("xc",), writes=("xc",))
            for k in range(16):
                vcopy(p, sq[:, k * 512:(k + 1) * 512], xc[:, k * 512:(k + 1) * 512],
                      reads=("xc",), writes=("sq",))
            mms(p, [(ps_l, ONESCOL_B, sq[:, k * 512:(k + 1) * 512], k == 0, k == 15)
                    for k in range(16)], reads=("sq", "cstb"), writes=("psl",))
            scopy(p, srow[:], ps_l, scale=1.0 / 2048, reads=("psl",), writes=("srow",))
            mms(p, [(ps_b[:], ONESROW_F, srow[:], True, True)],
                reads=("srow", "cst"), writes=("psb",))
            for k in range(16):
                vtt(p, xc[:, k * 512:(k + 1) * 512], xc[:, k * 512:(k + 1) * 512],
                    ps_b[:], ALU.subtract, reads=("psb", "xc"), writes=("xc",))
            for k in range(16):
                act(p, sq[:, k * 512:(k + 1) * 512], xc[:, k * 512:(k + 1) * 512],
                    AF.Square, reads=("xc",), writes=("sq",))
            mms(p, [(ps_l, ONESCOL_B, sq[:, k * 512:(k + 1) * 512], k == 0, k == 15)
                    for k in range(16)], reads=("sq", "cstb"), writes=("psl",))
            act(p, srow[:], ps_l, AF.Ln, scale=1.0 / 2048, bias=cst[0:1, 257:258],
                reads=("psl", "cst"), writes=("srow",))
            act(p, srow[:], srow[:], AF.Exp, scale=-0.5, reads=("srow",), writes=("srow",))
            mms(p, [(ps_b[:], ONESROW_F, srow[:], True, True)],
                reads=("srow", "cst"), writes=("psb",))
            if c < 2:
                for k in range(16):
                    vtt(p, xnb[:, k * 512:(k + 1) * 512], xc[:, k * 512:(k + 1) * 512],
                        ps_b[:], ALU.mult, reads=("xc", "psb"), writes=("xnb",))
            else:
                for k in range(16):
                    vtt(p, xnf32[:], xc[:, k * 512:(k + 1) * 512],
                        ps_b[:], ALU.mult, reads=("xc", "psb"), writes=("xnf32",))
                    vcopy(p, xnb[:, k * 512:(k + 1) * 512], xnf32[:],
                          reads=("xnf32",), writes=("xnb",))
                    p.dma("sp", io["xnf"].rearrange("(a p) t -> p a t", p=128)
                          [:, k, (c - 2) * 512:(c - 1) * 512],
                          xnf32[:], reads=("xnf32",), writes=("xnf",))
            p.dma("sp", xnr[:, :, c * 512:(c + 1) * 512],
                  xnb.ap().rearrange("p (a b) -> p a b", b=512),
                  reads=("xnb",), writes=("xnbuf",))
    p.barrier()

    kbmax = [12, 16]
    with ExitStack() as ph:
        xnt = ph.enter_context(nc.sbuf_tensor("xnt", [128, 16 * 1024], BF16))
        WS = [ph.enter_context(nc.sbuf_tensor(f"WS{i}", [128, 2048], BF16)) for i in range(2)]
        PT = ph.enter_context(nc.sbuf_tensor("PT", [128, 8 * 512], BF16))
        LB8 = ph.enter_context(nc.sbuf_tensor("LB8", [8, 512], F32))
        RB = ph.enter_context(nc.sbuf_tensor("RB", [128, 512], F32))

        for hg in range(2):
            hcol = hg * 1024
            for half in range(2):
                p.dma("sp", xnt.ap().rearrange("p (a t) -> p a t", t=1024),
                      xnr[:, :, half * 1024:(half + 1) * 1024],
                      reads=("xnt",), writes=("xnt",))
                for tg in range(2):
                    for dc in range(2):
                        for kt in range(16):
                            wn = f"ws{kt % 2}"
                            p.dma("gd", WS[kt % 2][:, 0:512],
                                  io["w_v"][kt * 128:(kt + 1) * 128,
                                            hcol + dc * 512:hcol + (dc + 1) * 512],
                                  reads=(wn,), writes=(wn,))
                            mms(p, [(ps[tt][:],
                                     xnt[:, kt * 1024 + (tg * 4 + tt) * 128:
                                         kt * 1024 + (tg * 4 + tt + 1) * 128],
                                     WS[kt % 2][:, 0:512], kt == 0, False)
                                    for tt in range(4)],
                                reads=(wn, "xnt"),
                                writes=tuple(f"ps{tt}" for tt in range(4)))
                        mms(p, [(ps[tt][:], ONESROW_B,
                                 bvrow[0:1, hcol + dc * 512:hcol + (dc + 1) * 512],
                                 False, True) for tt in range(4)],
                            reads=("bvrow", "cstb"),
                            writes=tuple(f"ps{tt}" for tt in range(4)))
                        for tt in range(4):
                            ttg = half * 8 + tg * 4 + tt
                            scopy(p, VB[:, ttg * 1024 + dc * 512:ttg * 1024 + (dc + 1) * 512],
                                  ps[tt][:], reads=(f"ps{tt}",), writes=("VB",))
                for proj, dst, dstn, own_only, bofs in (
                        ("w_q", QT, "QT", True, 0), ("w_k", KT, "KT", False, 16)):
                    if own_only and half == 0:
                        continue
                    for m in range(8):
                        cols = hcol + m * 128
                        wn = f"ws{m % 2}"
                        p.dma("gd", WS[m % 2].ap().rearrange("p (a c) -> p a c", c=128),
                              io[proj].rearrange("(a p) c -> p a c", p=128)[:, :, cols:cols + 128],
                              reads=(wn,), writes=(wn,))
                        for ci in range(2):
                            pi = ci % 2
                            mms(p, [(ps[pi][:], WS[m % 2][:, k * 128:(k + 1) * 128],
                                     xnt[:, k * 1024 + ci * 512:k * 1024 + (ci + 1) * 512],
                                     k == 0, k == 15) for k in range(16)],
                                reads=(wn, "xnt"), writes=(f"ps{pi}",))
                            dcol = (m * 1024 + ci * 512 if own_only
                                    else m * 2048 + half * 1024 + ci * 512)
                            scopy(p, dst[:, dcol:dcol + 512], ps[pi][:],
                                  bias=biasq[:, bofs + hg * 8 + m:bofs + hg * 8 + m + 1],
                                  reads=(f"ps{pi}", "biasq"), writes=(dstn,))
            for cc in range(2):
                nkt = kbmax[cc]
                for h in range(8):
                    qsl = QT[:, h * 1024 + cc * 512: h * 1024 + (cc + 1) * 512]
                    for khalf in range((nkt + 7) // 8):
                        kts = range(khalf * 8, min(nkt, khalf * 8 + 8))
                        for kt in kts:
                            pi = kt % 2
                            mms(p, [(ps[pi][:], KT[:, h * 2048 + kt * 128:h * 2048 + (kt + 1) * 128],
                                     qsl, True, True)],
                                reads=("KT", "QT"), writes=(f"ps{pi}",))
                            jd = kt - 8 - cc * 4
                            if 0 <= jd < 4:
                                vtt(p, ps[pi][:, jd * 128:(jd + 1) * 128],
                                    ps[pi][:, jd * 128:(jd + 1) * 128], TRIU0, ALU.add,
                                    reads=("cst", f"ps{pi}"), writes=(f"ps{pi}",))
                            act(p, PT[:, (kt % 8) * 512:(kt % 8 + 1) * 512], ps[pi][:], AF.Exp,
                                bias=TMCOL[:, kt:kt + 1], scale=float(1.0 / np.sqrt(np.float32(128))),
                                reads=(f"ps{pi}", "cst"), writes=(f"PT{kt % 8}",))
                            nj = min(kt - 8 - cc * 4, 4)
                            if nj > 0:
                                vmemset(p, PT[:, (kt % 8) * 512:(kt % 8) * 512 + nj * 128], 0.0,
                                        writes=(f"PT{kt % 8}",))
                        pts = tuple(f"PT{kt % 8}" for kt in kts)
                        mms(p, [(psl128[:], lsel[:, h * 128:(h + 1) * 128],
                                 PT[:, (kt % 8) * 512:(kt % 8 + 1) * 512],
                                 h == 0 and kt == 0, h == 7 and kt == nkt - 1) for kt in kts],
                            reads=pts + ("lsel",), writes=("psl",))
                        mms(p, [(ps[5][:], VB[:, kt * 1024 + h * 128:kt * 1024 + (h + 1) * 128],
                                 PT[:, (kt % 8) * 512:(kt % 8 + 1) * 512], kt == 0, kt == nkt - 1)
                                for kt in kts],
                            reads=pts + ("VB",), writes=("ps5",))
                    ac = (hg * 8 + h) * 1024 + cc * 512
                    scopy(p, ATT[:, ac:ac + 512], ps[5][:], reads=("ps5",), writes=("ATTu",))
                scopy(p, LB8[:], psl128[0:8, :], reads=("psl",), writes=("LB8",))
                act(p, LB8[:], LB8[:], AF.Ln, reads=("LB8",), writes=("LB8",))
                act(p, LB8[:], LB8[:], AF.Exp, scale=-1.0, reads=("LB8",), writes=("LB8",))
                for h in range(8):
                    mms(p, [(ps_b[:], oh8[:, h * 128:(h + 1) * 128], LB8[:], True, True)],
                        reads=("LB8", "oh8"), writes=("psb",))
                    scopy(p, RB[:], ps_b[:], reads=("psb",), writes=("RB",))
                    ac = (hg * 8 + h) * 1024 + cc * 512
                    vtt(p, ATT[:, ac:ac + 512], ATT[:, ac:ac + 512], RB[:], ALU.mult,
                        reads=("ATTu", "RB"), writes=("ATT",))
    p.barrier()

    with ExitStack() as ph:
        WO = ph.enter_context(nc.sbuf_tensor("WO", [128, 16 * 128], BF16))
        XRS = ph.enter_context(nc.sbuf_tensor("XRS", [128, 2 * 512], F32))
        OU = ph.enter_context(nc.sbuf_tensor("OU", [128, 2 * 512], F32))
        for m in range(16):
            p.dma("gd", WO.ap().rearrange("p (a c) -> p a c", c=128),
                  io["w_o"].rearrange("(a p) c -> p a c", p=128)[:, :, m * 128:(m + 1) * 128],
                  reads=("WO",), writes=("WO",))
            for cc in range(2):
                pi = cc % 2
                mms(p, [(ps[pi][:], WO[:, k * 128:(k + 1) * 128],
                         ATT[:, k * 1024 + cc * 512:k * 1024 + (cc + 1) * 512],
                         k == 0, k == 15) for k in range(16)],
                    reads=("WO", "ATT"), writes=(f"ps{pi}",))
                p.dma("sp", XRS[:, cc * 512:(cc + 1) * 512],
                      xr[:, m, 1024 + cc * 512:1024 + (cc + 1) * 512],
                      reads=(f"xrs{cc}",), writes=(f"xrs{cc}",))
                vstt(p, OU[:, cc * 512:(cc + 1) * 512], ps[pi][:],
                     biasq[:, 32 + m:33 + m], XRS[:, cc * 512:(cc + 1) * 512],
                     ALU.add, ALU.add,
                     reads=(f"ps{pi}", f"xrs{cc}", "biasq"), writes=(f"ou{cc}",))
                p.dma("sp", io["p1"][m * 128:(m + 1) * 128, cc * 512:(cc + 1) * 512],
                      OU[:, cc * 512:(cc + 1) * 512], reads=(f"ou{cc}",), writes=("p1",))


def _roll(xb, chunk):
    if chunk == 0:
        return np.concatenate([xb[1024:], xb[:1024]], axis=0)
    return xb


def host_l0a_inputs(inputs, core):
    b, chunk = core // 2, core % 2
    tri = np.where(np.arange(128)[:, None] <= np.arange(128)[None, :], 0.0, NEG)
    tm = np.zeros(16, np.float32)
    if chunk == 0:
        tm[:8] = NEG
    im = dict(
        xroll=_roll(inputs["L0_attn_query"][b], chunk).T,
        w_q=inputs["L0_attn_w_q"],
        w_k=inputs["L0_attn_w_k"],
        w_v=inputs["L0_attn_w_v"],
        w_o=inputs["L0_attn_w_o"],
        b_q_col=inputs["L0_attn_b_q"].reshape(16, 128).T,
        b_k_col=inputs["L0_attn_b_k"].reshape(16, 128).T,
        b_o_col=inputs["L0_attn_b_o"].reshape(16, 128).T,
        b_v_row=inputs["L0_attn_b_v"].reshape(1, 2048),
        triu0=tri,
        onescol=np.ones((128, 1), np.float32),
        onesrow=np.ones((1, 128), np.float32),
        tmcol=np.tile(tm[None, :], (128, 1)),
        epscol=np.full((128, 1), 1e-5, np.float32),
        oh8=np.concatenate([np.tile((np.arange(8) == h).astype(np.float32)[:, None], (1, 128))
                            for h in range(8)], axis=1),
    )
    im = {k: np.ascontiguousarray(v, dtype=np.float32) for k, v in im.items()}
    lsel = np.zeros((128, 1024), np.float32)
    for h in range(8):
        lsel[:, h * 128 + h] = 1.0
    im["lsel"] = lsel.astype(ml_dtypes.bfloat16)
    return im


def build_l0f():
    nc = bass.Bass("TRN2", target_bir_lowering=False, debug=False)
    io = {}
    for n, sh, dt, kind in [
        ("xnf", [D, TOK], F32, "ExternalInput"),
        ("p1", [D, TOK], F32, "ExternalInput"),
        ("gate_wT", [D, 8], F32, "ExternalInput"),
        ("ewgT", [8, D, 2816], F32, "ExternalInput"),
        ("ewdT", [8, 1408, D], F32, "ExternalInput"),
        ("swgT", [D, 5632], F32, "ExternalInput"),
        ("swdT", [2816, D], F32, "ExternalInput"),
        ("identf", [128, 128], F32, "ExternalInput"),
        ("oh8", [8, 1024], F32, "ExternalInput"),
        ("x1", [D, TOK], F32, "ExternalOutput"),
    ]:
        io[n] = nc.dram_tensor(n, sh, dt, kind=kind).ap()
    p = Prog(nc)
    with ExitStack() as es:
        p.setup(es)
        emit_moe(p, es, io)
        p.tail()
        block = es.enter_context(nc.Block())
        p.finalize(block)
    return nc


def emit_moe(p, es, io):
    nc = p.nc
    identf = es.enter_context(nc.sbuf_tensor("s_identf", [128, 128], F32))
    p.dma("sp", identf[:], io["identf"], writes=("identf",))
    oh8 = es.enter_context(nc.sbuf_tensor("s_oh8", [8, 1024], F32))
    p.dma("sp", oh8[:], io["oh8"], writes=("oh8",))
    gw = es.enter_context(nc.sbuf_tensor("gw", [128, 16 * 8], F32))
    p.dma("sp", gw.ap().rearrange("p (a c) -> p a c", c=8),
          io["gate_wT"].rearrange("(a p) c -> p a c", p=128), writes=("gw",))

    xnf = es.enter_context(nc.sbuf_tensor("sb_xnf", [128, 16 * 512], F32))
    xnb = es.enter_context(nc.sbuf_tensor("xnb", [128, 16 * 512], BF16))
    X1A = es.enter_context(nc.sbuf_tensor("X1A", [128, 16 * 512], F32))
    TM = es.enter_context(nc.sbuf_tensor("TM", [128, 22 * 512], BF16))
    CW8 = es.enter_context(nc.sbuf_tensor("CW8", [8, 512], F32))
    CWB = es.enter_context(nc.sbuf_tensor("CWB", [128, 512], F32))
    RT = es.enter_context(nc.sbuf_tensor("RT", [128, 8 * 10], F32))
    WG = [es.enter_context(nc.sbuf_tensor(f"WG{i}", [128, 2048], BF16)) for i in range(4)]
    WD = es.enter_context(nc.sbuf_tensor("WD", [128, 22 * 128], BF16))
    T1 = es.enter_context(nc.sbuf_tensor("T1", [128, 512], F32))
    T2 = es.enter_context(nc.sbuf_tensor("T2", [128, 512], F32))
    PR = es.enter_context(nc.sbuf_tensor("PR", [128, 512], F32))

    psg = es.enter_context(nc.psum_tensor("psg", [128, 512], F32))
    psp = es.enter_context(nc.psum_tensor("psp", [128, 512], F32))
    psd = [es.enter_context(nc.psum_tensor(f"psd{i}", [128, 512], F32)) for i in range(2)]
    psr = es.enter_context(nc.psum_tensor("psr", [128, 8], F32))
    pst = es.enter_context(nc.psum_tensor("pst", [8, 128], F32))
    pcw = es.enter_context(nc.psum_tensor("pcw", [128, 512], F32))

    xnfr = io["xnf"].rearrange("(a p) t -> p a t", p=128)
    p1r = io["p1"].rearrange("(a p) t -> p a t", p=128)
    x1r = io["x1"].rearrange("(a p) t -> p a t", p=128)

    for th in range(2):
        tsl = slice(th * 512, (th + 1) * 512)
        p.dma("sp", xnf.ap().rearrange("p (a b) -> p a b", b=512),
              xnfr[:, :, tsl], reads=("xnf",), writes=("xnf",))
        for k in range(16):
            vcopy(p, xnb[:, k * 512:(k + 1) * 512], xnf[:, k * 512:(k + 1) * 512],
                  reads=("xnf",), writes=("xnb",))
        for tt in range(4):
            mms(p, [(psr[:], xnf[:, k * 512 + tt * 128:k * 512 + (tt + 1) * 128],
                     gw[:, k * 8:(k + 1) * 8], k == 0, k == 15) for k in range(16)],
                reads=("xnf", "gw"), writes=("psr",))
            c0 = tt * 10
            SC = RT[:, c0:c0 + 8]
            NM = RT[:, c0 + 8:c0 + 9]
            LS = RT[:, c0 + 9:c0 + 10]
            p.op("ve", lambda v, cb, SC=SC, NM=NM: cb(v.tensor_reduce(
                NM, psr[:], AX.X, ALU.max, negate=True)), reads=("psr",), writes=("rt",))
            act(p, SC, psr[:], AF.Exp, bias=NM, accum=LS,
                reads=("psr", "rt"), writes=("rt",))
            vrecip(p, NM, LS, reads=("rt",), writes=("rt",))
            vts(p, SC, SC, NM, ALU.mult, reads=("rt",), writes=("rt",))
            vreduce(p, LS, SC, ALU.max, reads=("rt",), writes=("rt",))
            vts(p, T1[:, 0:8], SC, LS, ALU.is_ge, reads=("rt",), writes=("t1",))
            vstt(p, T2[:, 0:8], T1[:, 0:8], NEG, SC, ALU.mult, ALU.add,
                 reads=("t1", "rt"), writes=("t2",))
            vreduce(p, NM, T2[:, 0:8], ALU.max, reads=("t2",), writes=("rt",))
            vtt(p, LS, LS, NM, ALU.add, reads=("rt",), writes=("rt",))
            vts(p, LS, LS, 1e-20, ALU.add, reads=("rt",), writes=("rt",))
            vts(p, T1[:, 0:8], SC, NM, ALU.is_ge, reads=("rt",), writes=("t1",))
            vrecip(p, NM, LS, reads=("rt",), writes=("rt",))
            vtt(p, SC, SC, T1[:, 0:8], ALU.mult, reads=("rt", "t1"), writes=("rt",))
            vts(p, SC, SC, NM, ALU.mult, reads=("rt",), writes=("rt",))
            p.op("pe", lambda t, cb, SC=SC: cb(t.transpose(pst[:], SC, identf[:])),
                 reads=("rt", "identf"), writes=("pst",))
            scopy(p, CW8[:, tt * 128:(tt + 1) * 128], pst[:], reads=("pst",), writes=("CW8",))
        nsl_e, nsl_s = 11, 22
        for e in range(9):
            shared = e == 8
            nsl = nsl_s if shared else nsl_e
            ioff = 1408 if not shared else 2816
            if not shared:
                mms(p, [(pcw[:], oh8[:, e * 128:(e + 1) * 128], CW8[:], True, True)],
                    reads=("CW8", "oh8"), writes=("pcw",))
                scopy(p, CWB[:], pcw[:], reads=("pcw",), writes=("CWB",))
            for i in range(nsl):
                wa, wb = WG[(2 * i) % 4], WG[(2 * i + 1) % 4]
                na, nb = f"wg{(2 * i) % 4}", f"wg{(2 * i + 1) % 4}"
                if shared:
                    src_g = io["swgT"].rearrange("(a p) c -> p a c", p=128)[:, :, i * 128:(i + 1) * 128]
                    src_p = io["swgT"].rearrange("(a p) c -> p a c", p=128)[:, :, ioff + i * 128:ioff + (i + 1) * 128]
                else:
                    src_g = io["ewgT"][e].rearrange("(a p) c -> p a c", p=128)[:, :, i * 128:(i + 1) * 128]
                    src_p = io["ewgT"][e].rearrange("(a p) c -> p a c", p=128)[:, :, ioff + i * 128:ioff + (i + 1) * 128]
                p.dma("gd", wa.ap().rearrange("p (a c) -> p a c", c=128), src_g,
                      reads=(na,), writes=(na,))
                p.dma("gd", wb.ap().rearrange("p (a c) -> p a c", c=128), src_p,
                      reads=(nb,), writes=(nb,))
                mms(p, [(psg[:], wa[:, k * 128:(k + 1) * 128], xnb[:, k * 512:(k + 1) * 512],
                         k == 0, k == 15) for k in range(16)],
                    reads=(na, "xnb"), writes=("psg",))
                mms(p, [(psp[:], wb[:, k * 128:(k + 1) * 128], xnb[:, k * 512:(k + 1) * 512],
                         k == 0, k == 15) for k in range(16)],
                    reads=(nb, "xnb"), writes=("psp",))
                act(p, T1[:], psg[:], AF.Silu, reads=("psg",), writes=("t1",))
                vtt(p, T2[:], T1[:], psp[:], ALU.mult, reads=("t1", "psp"), writes=("t2",))
                if shared:
                    vcopy(p, TM[:, i * 512:(i + 1) * 512], T2[:],
                          reads=("t2",), writes=(f"tm{i}",))
                else:
                    vtt(p, TM[:, i * 512:(i + 1) * 512], T2[:], CWB[:], ALU.mult,
                        reads=("t2", "CWB"), writes=(f"tm{i}",))
            for m in range(16):
                if shared:
                    src_d = io["swdT"].rearrange("(a p) c -> p a c", p=128)[:, :, m * 128:(m + 1) * 128]
                else:
                    src_d = io["ewdT"][e].rearrange("(a p) c -> p a c", p=128)[:, :, m * 128:(m + 1) * 128]
                p.dma("gd", WD.ap().rearrange("p (a c) -> p a c", c=128)[:, 0:nsl, :], src_d,
                      reads=("WD",), writes=("WD",))
                pd = psd[m % 2]
                pdn = f"psd{m % 2}"
                mms(p, [(pd[:], WD[:, k * 128:(k + 1) * 128], TM[:, k * 512:(k + 1) * 512],
                         k == 0, k == nsl - 1) for k in range(nsl)],
                    reads=tuple(f"tm{k}" for k in range(nsl)) + ("WD",), writes=(pdn,))
                if e == 0:
                    scopy(p, X1A[:, m * 512:(m + 1) * 512], pd[:],
                          reads=(pdn,), writes=(f"x1a{m}",))
                else:
                    vtt(p, X1A[:, m * 512:(m + 1) * 512], X1A[:, m * 512:(m + 1) * 512],
                        pd[:], ALU.add, reads=(pdn,), writes=(f"x1a{m}",))
        for m in range(16):
            p.dma("sp", PR[:], p1r[:, m, tsl], reads=("PR",), writes=("PR",))
            vtt(p, X1A[:, m * 512:(m + 1) * 512], X1A[:, m * 512:(m + 1) * 512],
                PR[:], ALU.add, reads=("PR",), writes=(f"x1a{m}",))
            p.dma("sp", x1r[:, m, tsl], X1A[:, m * 512:(m + 1) * 512],
                  reads=(f"x1a{m}",), writes=("x1",))


def host_l0f_inputs(inputs, core, xnf, p1):
    im = dict(
        xnf=xnf, p1=p1,
        gate_wT=inputs["L0_ffn_gate_w"].T,
        ewgT=np.ascontiguousarray(np.transpose(inputs["L0_ffn_expert_wg"], (0, 2, 1))),
        ewdT=np.ascontiguousarray(np.transpose(inputs["L0_ffn_expert_wd"], (0, 2, 1))),
        swgT=inputs["L0_ffn_shared_wg"].T,
        swdT=inputs["L0_ffn_shared_wd"].T,
        identf=np.eye(128, dtype=np.float32),
        oh8=np.concatenate([np.tile((np.arange(8) == h).astype(np.float32)[:, None], (1, 128))
                            for h in range(8)], axis=1),
    )
    return {k: np.ascontiguousarray(v, dtype=np.float32) for k, v in im.items()}


def build_l1f():
    nc = bass.Bass("TRN2", target_bir_lowering=False, debug=False)
    io = {}
    for n, sh, dt, kind in [
        ("x1own", [D, TOK], F32, "ExternalInput"),
        ("p2tm", [TOK, D], F32, "ExternalInput"),
        ("w_upT", [D, 4096], F32, "ExternalInput"),
        ("w_downT", [4096, D], F32, "ExternalInput"),
        ("onescol", [128, 1], F32, "ExternalInput"),
        ("onesrow", [1, 128], F32, "ExternalInput"),
        ("epscol", [128, 1], F32, "ExternalInput"),
        ("identb", [128, 128], BF16, "ExternalInput"),
        ("ternu", [D, 4096], BF16, "Internal"),
        ("ternd", [4096, D], BF16, "Internal"),
        ("hscratch", [TOK, 4096], F32, "Internal"),
        ("out", [TOK, D], F32, "ExternalOutput"),
    ]:
        io[n] = nc.dram_tensor(n, sh, dt, kind=kind).ap()
    p = Prog(nc)
    with ExitStack() as es:
        p.setup(es)
        emit_l1f(p, es, io)
        p.tail()
        block = es.enter_context(nc.Block())
        p.finalize(block)
    return nc


def emit_l1f(p, es, io):
    nc = p.nc
    cst = es.enter_context(nc.sbuf_tensor("c_f", [128, 132], F32))
    cstb = es.enter_context(nc.sbuf_tensor("c_b", [128, 129], BF16))
    p.dma("sp", cst[:, 0:1], io["onescol"], writes=("cst",))
    p.dma("sp", cst[0:1, 1:129], io["onesrow"], writes=("cst",))
    p.dma("sp", cst[:, 129:130], io["epscol"], writes=("cst",))
    p.dma("gd", cstb[:, 0:1], io["onescol"], writes=("cstb",))
    p.dma("gd", cstb[:, 1:129], io["identb"], writes=("cstb",))
    ONESCOL_F, ONESROW_F, EPSCOL = cst[:, 0:1], cst[0:1, 1:129], cst[:, 129:130]
    ONESCOL_B, IDB = cstb[:, 0:1], cstb[:, 1:129]

    SR = es.enter_context(nc.sbuf_tensor("SR", [128, 6], F32))
    psS = es.enter_context(nc.psum_tensor("psS", [1, 512], F32))
    psB = es.enter_context(nc.psum_tensor("psB", [128, 512], F32))

    with ExitStack() as ph:
        WF = ph.enter_context(nc.sbuf_tensor("WF", [128, 4096], F32))
        WA = ph.enter_context(nc.sbuf_tensor("WA", [128, 4096], BF16))
        WT = ph.enter_context(nc.sbuf_tensor("WT", [128, 4096], BF16))
        SRW = ph.enter_context(nc.sbuf_tensor("SRW", [1, 512], F32))
        for wi, (nm, hbm, nt, nq) in enumerate((("u", io["w_upT"], 16, 8),
                                                ("d", io["w_downT"], 32, 4))):
            for rt in range(nt):
                p.dma("sp", WF[:, 0:nq * 512], hbm[rt * 128:(rt + 1) * 128, :],
                      reads=("WF",), writes=("WF",))
                for q in range(nq):
                    act(p, WA[:, q * 512:(q + 1) * 512], WF[:, q * 512:(q + 1) * 512],
                        AF.Abs, reads=("WF",), writes=("WA",))
                mms(p, [(psS[:], ONESCOL_B, WA[:, q * 512:(q + 1) * 512],
                         rt == 0 and q == 0, rt == nt - 1 and q == nq - 1)
                        for q in range(nq)],
                    reads=("WA", "cstb"), writes=("psS",))
            scopy(p, SRW[:], psS[:], reads=("psS",), writes=("SRW",))
            vreduce(p, SRW[0:1, 0:1], SRW[:], ALU.add, reads=("SRW",), writes=("SRWr",))
            vts(p, SRW[0:1, 0:1], SRW[0:1, 0:1], 1.0 / float(2048 * 4096), ALU.mult,
                1e-5, ALU.max, reads=("SRWr",), writes=("SRWr",))
            mms(p, [(psB[:, 0:1], ONESROW_F, SRW[0:1, 0:1], True, True)],
                reads=("SRWr", "cst"), writes=("psB",))
            scopy(p, SR[:, wi:wi + 1], psB[:, 0:1], scale=0.5,
                  reads=("psB",), writes=("SR",))
            scopy(p, SR[:, 2 + wi:3 + wi], psB[:, 0:1], reads=("psB",), writes=("SR",))
            scopy(p, SR[:, 4 + wi:5 + wi], psB[:, 0:1], scale=-0.5,
                  reads=("psB",), writes=("SR",))
        for wi, (nm, hbm, dst, nt, nq) in enumerate(
                (("u", io["w_upT"], io["ternu"], 16, 8),
                 ("d", io["w_downT"], io["ternd"], 32, 4))):
            hs = SR[:, wi:wi + 1]
            nhs = SR[:, 4 + wi:5 + wi]
            for rt in range(nt):
                p.dma("sp", WF[:, 0:nq * 512], hbm[rt * 128:(rt + 1) * 128, :],
                      reads=("WF",), writes=("WF",))
                for q in range(nq):
                    sl = slice(q * 512, (q + 1) * 512)
                    vts(p, WA[:, sl], WF[:, sl], hs, ALU.is_ge,
                        reads=("WF", "SR"), writes=("WA",))
                    vts(p, WT[:, sl], WF[:, sl], nhs, ALU.is_le,
                        reads=("WF", "SR"), writes=("WT",), eng="ge")
                    vtt(p, WT[:, sl], WA[:, sl], WT[:, sl], ALU.subtract,
                        reads=("WA", "WT"), writes=("WT",))
                p.dma("sp", dst[rt * 128:(rt + 1) * 128, :], WT[:, 0:nq * 512],
                      reads=("WT",), writes=(f"tern{nm}",))
    p.barrier()

    XC = es.enter_context(nc.sbuf_tensor("XC", [128, 16 * 512], F32))
    SQ = es.enter_context(nc.sbuf_tensor("SQ", [128, 16 * 512], BF16))
    K1 = es.enter_context(nc.sbuf_tensor("K1", [128, 16 * 512], BF16))
    WU = es.enter_context(nc.sbuf_tensor("WU", [128, 16 * 512], BF16))
    K2T = es.enter_context(nc.sbuf_tensor("K2T", [128, 32 * 512], BF16))
    HR = es.enter_context(nc.sbuf_tensor("HR", [128, 512], F32))
    Q1 = es.enter_context(nc.sbuf_tensor("Q1", [128, 512], F32))
    Q2 = es.enter_context(nc.sbuf_tensor("Q2", [128, 512], F32))
    QI = es.enter_context(nc.sbuf_tensor("QI", [128, 512], I32))
    K2M = es.enter_context(nc.sbuf_tensor("K2M", [128, 512], BF16))
    RQ = es.enter_context(nc.sbuf_tensor("RQ", [128, 16], F32))
    R1 = es.enter_context(nc.sbuf_tensor("R1", [1, 512], F32))
    R2 = es.enter_context(nc.sbuf_tensor("R2", [1, 512], F32))
    SCB = es.enter_context(nc.sbuf_tensor("SCB", [128, 512], F32))
    P2B = es.enter_context(nc.sbuf_tensor("P2B", [128, 512], F32))
    OB = es.enter_context(nc.sbuf_tensor("OB", [128, 512], F32))
    psh = [es.enter_context(nc.psum_tensor(f"psh{i}", [128, 512], F32)) for i in range(4)]
    pstp = es.enter_context(nc.psum_tensor("pstp", [128, 1024], BF16))

    x1r = io["x1own"].rearrange("(a p) t -> p a t", p=128)
    teu = io["ternu"].rearrange("(a p) c -> p a c", p=128)
    hscr = io["hscratch"]

    for ch in range(2):
        tsl = slice(ch * 512, (ch + 1) * 512)
        p.dma("sp", XC.ap().rearrange("p (a b) -> p a b", b=512), x1r[:, :, tsl],
              reads=("XC",), writes=("XC",))
        for k in range(16):
            vcopy(p, SQ[:, k * 512:(k + 1) * 512], XC[:, k * 512:(k + 1) * 512],
                  reads=("XC",), writes=("SQ",))
        mms(p, [(psS[:], ONESCOL_B, SQ[:, k * 512:(k + 1) * 512], k == 0, k == 15)
                for k in range(16)], reads=("SQ", "cstb"), writes=("psS",))
        scopy(p, R1[:], psS[:], scale=1.0 / 2048, reads=("psS",), writes=("R1",))
        mms(p, [(psB[:], ONESROW_F, R1[:], True, True)], reads=("R1", "cst"), writes=("psB",))
        for k in range(16):
            vtt(p, XC[:, k * 512:(k + 1) * 512], XC[:, k * 512:(k + 1) * 512],
                psB[:], ALU.subtract, reads=("psB",), writes=("XC",))
        for k in range(16):
            act(p, SQ[:, k * 512:(k + 1) * 512], XC[:, k * 512:(k + 1) * 512],
                AF.Square, reads=("XC",), writes=("SQ",))
        mms(p, [(psS[:], ONESCOL_B, SQ[:, k * 512:(k + 1) * 512], k == 0, k == 15)
                for k in range(16)], reads=("SQ", "cstb"), writes=("psS",))
        act(p, R1[:], psS[:], AF.Ln, scale=1.0 / 2048, bias=EPSCOL[0:1, 0:1],
            reads=("psS", "cst"), writes=("R1",))
        act(p, R1[:], R1[:], AF.Exp, scale=-0.5, reads=("R1",), writes=("R1",))
        mms(p, [(psB[:], ONESROW_F, R1[:], True, True)], reads=("R1", "cst"), writes=("psB",))
        for k in range(16):
            vtt(p, XC[:, k * 512:(k + 1) * 512], XC[:, k * 512:(k + 1) * 512],
                psB[:], ALU.mult, reads=("psB",), writes=("XC",))
        act(p, HR[:], XC[:, 0:512], AF.Abs, reads=("XC",), writes=("HR",))
        for k in range(1, 16):
            act(p, Q1[:], XC[:, k * 512:(k + 1) * 512], AF.Abs,
                reads=("XC",), writes=("Q1",))
            vtt(p, HR[:], HR[:], Q1[:], ALU.max, reads=("Q1", "HR"), writes=("HR",))
        p.op("ge", lambda g, cb: cb(g.tensor_reduce(R1[:], HR[:], AX.C, ALU.max)),
             reads=("HR",), writes=("R1",))
        vts(p, R1[:], R1[:], 1e-5, ALU.max, reads=("R1",), writes=("R1",))
        vrecip(p, R2[:], R1[:], reads=("R1",), writes=("R2",))
        vts(p, R2[:], R2[:], 127.0, ALU.mult, reads=("R2",), writes=("R2",))
        mms(p, [(psB[:], ONESROW_F, R2[:], True, True)], reads=("R2", "cst"), writes=("psB",))
        scopy(p, SCB[:], psB[:], reads=("psB",), writes=("SCB",))
        for k in range(16):
            sl = slice(k * 512, (k + 1) * 512)
            vtt(p, Q1[:], XC[:, sl], SCB[:], ALU.mult, reads=("XC", "SCB"), writes=("Q1",))
            vts(p, Q1[:], Q1[:], 256.0, ALU.add, reads=("Q1",), writes=("Q1",))
            vcopy(p, QI[:], Q1[:], reads=("Q1",), writes=("QI",), eng="ge")
            scopy(p, Q2[:], QI[:], reads=("QI",), writes=("Q2",))
            vstt(p, HR[:], Q2[:], -1.0, Q1[:], ALU.mult, ALU.add,
                 reads=("Q2", "Q1"), writes=("HR",))
            vstt(p, Q2[:], HR[:], 0.5, Q2[:], ALU.is_gt, ALU.add,
                 reads=("HR", "Q2"), writes=("Q2",))
            vts(p, K1[:, sl], Q2[:], -256.0, ALU.add, reads=("Q2",), writes=("K1",))
        vts(p, R2[:], R1[:], 1.0 / 127.0, ALU.mult, reads=("R1",), writes=("R2",))
        for tt in range(4):
            mms(p, [(psh[0][:, 0:1], R2[0:1, tt * 128:(tt + 1) * 128], ONESCOL_F[0:1, 0:1],
                     True, True)], reads=("R2", "cst"), writes=("psh0",))
            scopy(p, RQ[:, tt:tt + 1], psh[0][:, 0:1], reads=("psh0",), writes=("RQ",))
            vtt(p, RQ[:, 4 + tt:5 + tt], RQ[:, tt:tt + 1], SR[:, 2:3], ALU.mult,
                reads=("RQ", "SR"), writes=("RQ",))
            act(p, RQ[:, 4 + tt:5 + tt], RQ[:, 4 + tt:5 + tt], AF.Square,
                reads=("RQ",), writes=("RQ",))
        for hc in range(8):
            p.dma("gd", WU.ap().rearrange("p (a b) -> p a b", b=512),
                  teu[:, :, hc * 512:(hc + 1) * 512], reads=("WU",), writes=("WU",))
            for tt in range(4):
                mms(p, [(psh[tt][:], K1[:, k * 512 + tt * 128:k * 512 + (tt + 1) * 128],
                         WU[:, k * 512:(k + 1) * 512], k == 0, k == 15) for k in range(16)],
                    reads=("WU", "K1"), writes=(f"psh{tt}",))
                vts(p, HR[:], psh[tt][:], 0.0, ALU.max, reads=(f"psh{tt}",), writes=("HR",))
                if hc == 0:
                    vreduce(p, RQ[:, 8 + tt:9 + tt], HR[:], ALU.max,
                            reads=("HR",), writes=("RQ",))
                else:
                    vreduce(p, TC[:, 0:1] if False else RQ[:, 12 + tt:13 + tt], HR[:],
                            ALU.max, reads=("HR",), writes=("RQm",))
                    vtt(p, RQ[:, 8 + tt:9 + tt], RQ[:, 8 + tt:9 + tt],
                        RQ[:, 12 + tt:13 + tt], ALU.max, reads=("RQm", "RQ"), writes=("RQ",))
                p.dma("sp", hscr[ch * 512 + tt * 128:ch * 512 + (tt + 1) * 128,
                                 hc * 512:(hc + 1) * 512], HR[:],
                      reads=("HR",), writes=(f"hs{tt}{hc}",))
        for tt in range(4):
            act(p, RQ[:, 12 + tt:13 + tt], RQ[:, 8 + tt:9 + tt], AF.Square,
                reads=("RQ",), writes=("RQ",))
            vtt(p, RQ[:, 12 + tt:13 + tt], RQ[:, 12 + tt:13 + tt], RQ[:, 4 + tt:5 + tt],
                ALU.mult, reads=("RQ",), writes=("RQ",))
            vts(p, RQ[:, 12 + tt:13 + tt], RQ[:, 12 + tt:13 + tt], 1e-5, ALU.max,
                reads=("RQ",), writes=("RQ",))
            vrecip(p, RQ[:, 8 + tt:9 + tt], RQ[:, 12 + tt:13 + tt],
                   reads=("RQ",), writes=("RQ",))
            vstt(p, RQ[:, 8 + tt:9 + tt], RQ[:, 8 + tt:9 + tt], 127.0,
                 RQ[:, 4 + tt:5 + tt], ALU.mult, ALU.mult, reads=("RQ",), writes=("RQ",))
            vstt(p, RQ[:, 12 + tt:13 + tt], RQ[:, 12 + tt:13 + tt], 1.0 / 127.0,
                 SR[:, 3:4], ALU.mult, ALU.mult, reads=("RQ", "SR"), writes=("RQ",))
        for tt in range(4):
            for hc in range(8):
                p.dma("sp", HR[:], hscr[ch * 512 + tt * 128:ch * 512 + (tt + 1) * 128,
                                        hc * 512:(hc + 1) * 512],
                      reads=(f"hs{tt}{hc}", "HR"), writes=("HR",))
                act(p, Q1[:], HR[:], AF.Square, reads=("HR",), writes=("Q1",))
                vts(p, Q1[:], Q1[:], RQ[:, 8 + tt:9 + tt], ALU.mult,
                    reads=("Q1", "RQ"), writes=("Q1",))
                vcopy(p, QI[:], Q1[:], reads=("Q1",), writes=("QI",), eng="ge")
                scopy(p, Q2[:], QI[:], reads=("QI",), writes=("Q2",))
                vstt(p, HR[:], Q2[:], -1.0, Q1[:], ALU.mult, ALU.add,
                     reads=("Q2", "Q1"), writes=("HR",))
                vstt(p, Q2[:], HR[:], 0.5, Q2[:], ALU.is_gt, ALU.add,
                     reads=("HR", "Q2"), writes=("Q2",))
                vcopy(p, K2M[:], Q2[:], reads=("Q2",), writes=("K2M",), eng="ge")
                for q in range(4):
                    p.op("pe", lambda t, cb, q=q: cb(t.transpose(
                        pstp[:, q * 128:(q + 1) * 128], K2M[:, q * 128:(q + 1) * 128],
                        IDB)), reads=("K2M", "cstb"), writes=("pstp",))
                scopy(p, K2T.ap().rearrange("p (a b) -> p a b", b=512)
                      [:, hc * 4:(hc + 1) * 4, tt * 128:(tt + 1) * 128],
                      pstp[:, 0:512].rearrange("p (a b) -> p a b", b=128),
                      reads=("pstp",), writes=(f"k2t{hc}",))
        for oc in range(4):
            for half in range(2):
                p.dma("gd", WU.ap().rearrange("p (a b) -> p a b", b=512),
                      io["ternd"].rearrange("(a p) c -> p a c", p=128)
                      [:, half * 16:(half + 1) * 16, oc * 512:(oc + 1) * 512],
                      reads=("WU",), writes=("WU",))
                for tt in range(4):
                    mms(p, [(psh[tt][:],
                             K2T[:, (half * 16 + k) * 512 + tt * 128:
                                 (half * 16 + k) * 512 + (tt + 1) * 128],
                             WU[:, k * 512:(k + 1) * 512],
                             half == 0 and k == 0, half == 1 and k == 15)
                            for k in range(16)],
                        reads=("WU",) + tuple(f"k2t{h2}" for h2 in range(8)),
                        writes=(f"psh{tt}",))
            for tt in range(4):
                p.dma("sp", P2B[:], io["p2tm"][ch * 512 + tt * 128:ch * 512 + (tt + 1) * 128,
                                               oc * 512:(oc + 1) * 512],
                      reads=("P2B",), writes=("P2B",))
                vstt(p, OB[:], psh[tt][:], RQ[:, 12 + tt:13 + tt], P2B[:],
                     ALU.mult, ALU.add, reads=(f"psh{tt}", "RQ", "P2B"), writes=("OB",))
                p.dma("sp", io["out"][ch * 512 + tt * 128:ch * 512 + (tt + 1) * 128,
                                      oc * 512:(oc + 1) * 512], OB[:],
                      reads=("OB",), writes=("out",))


def host_l1f_inputs(inputs, core, x1own, p2tm):
    im = dict(
        x1own=x1own, p2tm=p2tm,
        w_upT=inputs["L1_ffn_w_up"].T,
        w_downT=inputs["L1_ffn_w_down"].T,
        onescol=np.ones((128, 1), np.float32),
        onesrow=np.ones((1, 128), np.float32),
        epscol=np.full((128, 1), 1e-5, np.float32),
    )
    im = {k: np.ascontiguousarray(v, dtype=np.float32) for k, v in im.items()}
    im["identb"] = np.eye(128, dtype=np.float32).astype(ml_dtypes.bfloat16)
    return im


def build_l1a(dbg=False):
    nc = bass.Bass("TRN2", target_bir_lowering=False, debug=False)
    io = {}
    for n, sh, dt, kind in [
        ("x1roll", [D, T], F32, "ExternalInput"),
        ("w_qc", [D, 1024], F32, "ExternalInput"),
        ("w_qup", [1024, D], F32, "ExternalInput"),
        ("w_kv_b", [D, 128], F32, "ExternalInput"),
        ("w_z_b", [D, 128], F32, "ExternalInput"),
        ("w_k", [D, 128], F32, "ExternalInput"),
        ("w_v", [D, 128], F32, "ExternalInput"),
        ("wi_kv", [D, 64], F32, "ExternalInput"),
        ("wi_z", [D, 64], F32, "ExternalInput"),
        ("w_dq", [D, 64], F32, "ExternalInput"),
        ("w_iuq", [64, 256], F32, "ExternalInput"),
        ("w_w", [D, 4], F32, "ExternalInput"),
        ("biasbT", [128, T], F32, "ExternalInput"),
        ("biasiT", [64, T], F32, "ExternalInput"),
        ("qnormcol", [128, 1], F32, "ExternalInput"),
        ("knormcol", [128, 1], F32, "ExternalInput"),
        ("sinkcol", [16, 1], F32, "ExternalInput"),
        ("group_w", [4, 512, 512], F32, "ExternalInput"),
        ("final_w", [D, D], F32, "ExternalInput"),
        ("c128", [128, TOK], F32, "ExternalInput"),
        ("s128", [128, TOK], F32, "ExternalInput"),
        ("ck128", [128, T], F32, "ExternalInput"),
        ("sk128", [128, T], F32, "ExternalInput"),
        ("permrope", [128, 128], F32, "ExternalInput"),
        ("validmul", [128, 8 * 64], F32, "ExternalInput"),
        ("tbrow", [128, 64], F32, "ExternalInput"),
        ("hpcol", [128, 8], F32, "ExternalInput"),
        ("tmsw", [128, 48], F32, "ExternalInput"),
        ("wintri", [128, 128], F32, "ExternalInput"),
        ("diagtri", [128, 128], F32, "ExternalInput"),
        ("lsel16", [128, 16 * 128], BF16, "ExternalInput"),
        ("oh16", [16, 16 * 128], F32, "ExternalInput"),
        ("onescol", [128, 1], F32, "ExternalInput"),
        ("onesrow", [1, 128], F32, "ExternalInput"),
        ("epscol", [128, 1], F32, "ExternalInput"),
        ("eps6col", [128, 1], F32, "ExternalInput"),
        ("m50col", [16, 1], F32, "ExternalInput"),
        ("sinkrow", [1, 16], F32, "ExternalInput"),
        ("q128eps", [16, 1], F32, "ExternalInput"),
        ("identb", [128, 128], BF16, "ExternalInput"),
        ("qcscr", [1024, 1024], BF16, "Internal"),
        ("p2", [D, TOK], F32, "ExternalOutput"),
    ]:
        io[n] = nc.dram_tensor(n, sh, dt, kind=kind).ap()
    if dbg:
        for n, sh in (("dISC", [128, 512]), ("dSELA", [128, 512]),
                      ("dAT", [128, 16 * 1024]), ("dKVN", [128, 64]), ("dQT", [128, 16 * 1024]),
                      ("dSWOB", [128, 16 * 128]), ("dRLB", [16, 128]),
                      ("dPWB", [128, 704]), ("dPS0", [128, 128]),
                      ("dRB", [128, 128]), ("dSWC", [128, 128]), ("dLR", [1, 128]), ("dPSL", [1, 128]),
                      ("dATs", [128, 16 * 128])):
            io[n] = nc.dram_tensor(n, sh, F32, kind="ExternalOutput").ap()
    io["dbg"] = dbg
    p = Prog(nc)
    with ExitStack() as es:
        p.setup(es)
        emit_l1a(p, es, io)
        p.tail()
        block = es.enter_context(nc.Block())
        p.finalize(block)
    return nc


def emit_l1a(p, es, io):
    nc = p.nc
    RS = float(1.0 / np.sqrt(np.float32(128)))
    cst = es.enter_context(nc.sbuf_tensor("c_f", [128, 8], F32))
    p.dma("sp", cst[:, 0:1], io["onescol"], writes=("cst",))
    p.dma("sp", cst[:, 2:3], io["epscol"], writes=("cst",))
    p.dma("sp", cst[:, 3:4], io["eps6col"], writes=("cst",))
    ONESCOL_F = cst[:, 0:1]
    orow = es.enter_context(nc.sbuf_tensor("orow", [1, 128], F32))
    p.dma("sp", orow[:], io["onesrow"], writes=("orow",))
    orb = es.enter_context(nc.sbuf_tensor("orb", [1, 128], BF16))
    p.dma("gd", orb[:], io["onesrow"], writes=("orb",))
    ocb = es.enter_context(nc.sbuf_tensor("ocb", [128, 1], BF16))
    p.dma("gd", ocb[:], io["onescol"], writes=("ocb",))
    idb = es.enter_context(nc.sbuf_tensor("idb", [128, 128], BF16))
    p.dma("sp", idb[:], io["identb"], writes=("idb",))
    prm = es.enter_context(nc.sbuf_tensor("prm", [128, 128], BF16))
    p.dma("gd", prm[:], io["permrope"], writes=("prm",))
    lsel = es.enter_context(nc.sbuf_tensor("lsel", [128, 16 * 128], BF16))
    p.dma("sp", lsel[:], io["lsel16"], writes=("lsel",))
    oh16 = es.enter_context(nc.sbuf_tensor("oh16t", [16, 16 * 128], F32))
    p.dma("sp", oh16[:], io["oh16"], writes=("oh16",))
    masks = es.enter_context(nc.sbuf_tensor("masks", [128, 64 + 512 + 8 + 48 + 256 + 2], F32))
    MV = masks[:, 64:576]
    TB = masks[:, 0:64]
    HP = masks[:, 576:584]
    TMSW = masks[:, 584:632]
    WTRI = masks[:, 632:760]
    DTRI = masks[:, 760:888]
    QNC = masks[:, 888:889]
    KNC = masks[:, 889:890]
    p.dma("sp", TB, io["tbrow"], writes=("masks",))
    p.dma("sp", MV, io["validmul"], writes=("masks",))
    p.dma("sp", HP, io["hpcol"], writes=("masks",))
    p.dma("sp", TMSW, io["tmsw"], writes=("masks",))
    p.dma("sp", WTRI, io["wintri"], writes=("masks",))
    p.dma("sp", DTRI, io["diagtri"], writes=("masks",))
    p.dma("sp", QNC, io["qnormcol"], writes=("masks",))
    p.dma("sp", KNC, io["knormcol"], writes=("masks",))
    snk = es.enter_context(nc.sbuf_tensor("snk", [16, 2], F32))
    p.dma("sp", snk[:, 0:1], io["sinkcol"], writes=("snk",))
    p.dma("sp", snk[:, 1:2], io["m50col"], writes=("snk",))
    snkq = es.enter_context(nc.sbuf_tensor("snkq", [16, 1], F32))
    p.dma("sp", snkq[:], io["q128eps"], writes=("snkq",))
    WSM = es.enter_context(nc.sbuf_tensor("WSM", [128, 16 * (128 + 128 + 128 + 128 + 64 + 64 + 64) + 256 + 64], BF16))
    ofs = {}
    off = 0
    for nm, w in (("kvb", 128), ("zb", 128), ("wk", 128), ("wv", 128),
                  ("ikv", 64), ("iz", 64), ("dq", 64)):
        ofs[nm] = off
        p.dma("gd", WSM[:, off:off + 16 * w].rearrange("p (a c) -> p a c", c=w),
              io[{"kvb": "w_kv_b", "zb": "w_z_b", "wk": "w_k", "wv": "w_v",
                  "ikv": "wi_kv", "iz": "wi_z", "dq": "w_dq"}[nm]]
              .rearrange("(a p) c -> p a c", p=128),
              writes=("WSM",))
        off += 16 * w
    ofs["iuq"] = off
    p.dma("gd", WSM[:, off:off + 256][0:64, :], io["w_iuq"], writes=("WSM",))
    off += 256
    ofs["ww"] = off
    p.dma("gd", WSM[:, off:off + 64].rearrange("p (a c) -> p a c", c=4),
          io["w_w"].rearrange("(a p) c -> p a c", p=128), writes=("WSM",))

    rope_c = es.enter_context(nc.sbuf_tensor("ropec", [128, 2 * 1024 + 2 * 2048], BF16))
    CQ, SQr = rope_c[:, 0:1024], rope_c[:, 1024:2048]
    CK, SK = rope_c[:, 2048:4096], rope_c[:, 4096:6144]
    p.dma("gd", CQ, io["c128"], writes=("ropec",))
    p.dma("gd", SQr, io["s128"], writes=("ropec",))
    p.dma("gd", CK, io["ck128"], writes=("ropec",))
    p.dma("gd", SK, io["sk128"], writes=("ropec",))

    SWKR = es.enter_context(nc.sbuf_tensor("SWKR", [128, 2048], BF16))
    SWV = es.enter_context(nc.sbuf_tensor("SWVb", [128, 16 * 128], BF16))
    CBF = es.enter_context(nc.sbuf_tensor("CBF", [128, 2048], BF16))
    ZBF = es.enter_context(nc.sbuf_tensor("ZBF", [128, 2048], BF16))
    CIF = es.enter_context(nc.sbuf_tensor("CIF", [64, 2048], BF16))
    ZIF = es.enter_context(nc.sbuf_tensor("ZIF", [64, 2048], BF16))
    DQF = es.enter_context(nc.sbuf_tensor("DQF", [64, 1024], BF16))
    QIT = es.enter_context(nc.sbuf_tensor("QIT", [64, 4 * 1024], BF16))
    WGT = es.enter_context(nc.sbuf_tensor("WGT", [128, 8 * 4], F32))
    ISC = es.enter_context(nc.sbuf_tensor("ISC", [128, 8 * 64], F32))
    SELA = es.enter_context(nc.sbuf_tensor("SELA", [128, 8 * 64], F32))
    KVN = es.enter_context(nc.sbuf_tensor("KVN", [128, 64 + 128], BF16))
    CKI = es.enter_context(nc.sbuf_tensor("CKI", [64, 64], BF16))
    SWKF = es.enter_context(nc.sbuf_tensor("SWKF", [128, 2048], F32))
    T0 = es.enter_context(nc.sbuf_tensor("T0", [128, 2048], F32))
    T1b = es.enter_context(nc.sbuf_tensor("T1b", [128, 2048], F32))
    LBS = es.enter_context(nc.sbuf_tensor("LBS", [16, 1024], F32))

    ps = [es.enter_context(nc.psum_tensor(f"psa{i}", [128, 512], F32)) for i in range(4)]
    psl = es.enter_context(nc.psum_tensor("pslL", [128, 512], F32))
    psb = es.enter_context(nc.psum_tensor("psbL", [128, 512], F32))
    pst = es.enter_context(nc.psum_tensor("pstL", [128, 512], BF16))
    psw = es.enter_context(nc.psum_tensor("pswL", [128, 512], F32))

    xr = io["x1roll"].rearrange("(a p) t -> p a t", p=128)

    with ExitStack() as ph:
        XC = ph.enter_context(nc.sbuf_tensor("XCa", [128, 16 * 512], F32))
        SQ = ph.enter_context(nc.sbuf_tensor("SQa", [128, 16 * 512], BF16))
        SRO = ph.enter_context(nc.sbuf_tensor("SROa", [1, 512], F32))
        WQC = ph.enter_context(nc.sbuf_tensor("WQC", [128, 2048], BF16))
        for c in range(4):
            p.dma("sp", XC.ap().rearrange("p (a b) -> p a b", b=512), xr[:, :, c * 512:(c + 1) * 512],
                  reads=("XC",), writes=("XC",))
            for k in range(16):
                vcopy(p, SQ[:, k * 512:(k + 1) * 512], XC[:, k * 512:(k + 1) * 512],
                      reads=("XC",), writes=("SQ",))
            mms(p, [(psl[0:1, :], ocb[:], SQ[:, k * 512:(k + 1) * 512], k == 0, k == 15)
                    for k in range(16)], reads=("SQ", "ocb"), writes=("psl",))
            scopy(p, SRO[:], psl[0:1, :], scale=1.0 / 2048, reads=("psl",), writes=("SRO",))
            mms(p, [(psb[:], orow[:], SRO[:], True, True)], reads=("SRO", "orow"), writes=("psb",))
            for k in range(16):
                vtt(p, XC[:, k * 512:(k + 1) * 512], XC[:, k * 512:(k + 1) * 512],
                    psb[:], ALU.subtract, reads=("psb",), writes=("XC",))
            for k in range(16):
                act(p, SQ[:, k * 512:(k + 1) * 512], XC[:, k * 512:(k + 1) * 512],
                    AF.Square, reads=("XC",), writes=("SQ",))
            mms(p, [(psl[0:1, :], ocb[:], SQ[:, k * 512:(k + 1) * 512], k == 0, k == 15)
                    for k in range(16)], reads=("SQ", "ocb"), writes=("psl",))
            act(p, SRO[:], psl[0:1, :], AF.Ln, scale=1.0 / 2048, bias=cst[0:1, 2:3],
                reads=("psl", "cst"), writes=("SRO",))
            act(p, SRO[:], SRO[:], AF.Exp, scale=-0.5, reads=("SRO",), writes=("SRO",))
            mms(p, [(psb[:], orow[:], SRO[:], True, True)], reads=("SRO", "orow"), writes=("psb",))
            for k in range(16):
                vtt(p, SQ[:, k * 512:(k + 1) * 512], XC[:, k * 512:(k + 1) * 512],
                    psb[:], ALU.mult, reads=("psb", "XC"), writes=("SQ",))
            cs = slice(c * 512, (c + 1) * 512)
            for nm, dst, dstn, wdt, ncols in (("kvb", CBF, "CBF", 128, 128),
                                              ("zb", ZBF, "ZBF", 128, 128),
                                              ("ikv", CIF, "CIF", 64, 64),
                                              ("iz", ZIF, "ZIF", 64, 64)):
                mms(p, [(psw[0:ncols, :], WSM[:, ofs[nm] + k * wdt: ofs[nm] + k * wdt + ncols],
                         SQ[:, k * 512:(k + 1) * 512], k == 0, k == 15) for k in range(16)],
                    reads=("WSM", "SQ"), writes=("psw",))
                scopy(p, dst[0:ncols, cs], psw[0:ncols, :], reads=("psw",), writes=(dstn,))
            mms(p, [(psw[:], WSM[:, ofs["wk"] + k * 128: ofs["wk"] + (k + 1) * 128],
                     SQ[:, k * 512:(k + 1) * 512], k == 0, k == 15) for k in range(16)],
                reads=("WSM", "SQ"), writes=("psw",))
            scopy(p, SWKF[:, cs], psw[:], reads=("psw",), writes=("SWKF",))
            for tt in range(4):
                mms(p, [(ps[tt][0:128, 0:128],
                         SQ[:, k * 512 + tt * 128:k * 512 + (tt + 1) * 128],
                         WSM[:, ofs["wv"] + k * 128: ofs["wv"] + (k + 1) * 128],
                         k == 0, k == 15) for k in range(16)],
                    reads=("WSM", "SQ"), writes=(f"ps{tt}",))
                scopy(p, SWV[:, (c * 4 + tt) * 128:(c * 4 + tt + 1) * 128], ps[tt][:, 0:128],
                      reads=(f"ps{tt}",), writes=("SWV",))
            if c >= 2:
                co = slice((c - 2) * 512, (c - 1) * 512)
                for m in range(8):
                    p.dma("gd", WQC[:, 0:16 * 128].rearrange("p (a b) -> p a b", b=128),
                          io["w_qc"].rearrange("(a p) b -> p a b", p=128)[:, :, m * 128:(m + 1) * 128],
                          reads=("WQC",), writes=("WQC",))
                    mms(p, [(psw[:], WQC[:, k * 128:(k + 1) * 128],
                             SQ[:, k * 512:(k + 1) * 512], k == 0, k == 15) for k in range(16)],
                        reads=("WQC", "SQ"), writes=("psw",))
                    scopy(p, T1b[:, 0:512], psw[:], reads=("psw",), writes=("T1b",))
                    vcopy(p, T1b[0:128, 1024:1536].rearrange("p c -> p c") if False else T1b[:, 1024:1536], T1b[:, 0:512], reads=("T1b",), writes=("T1b",), eng="ge") if False else None
                    p.dma("gd", io["qcscr"][m * 128:(m + 1) * 128, (c - 2) * 512:(c - 1) * 512],
                          T1b[:, 0:512], reads=("T1b",), writes=("qcscr",))
                mms(p, [(psw[0:64, :], WSM[:, ofs["dq"] + k * 64: ofs["dq"] + k * 64 + 64],
                         SQ[:, k * 512:(k + 1) * 512], k == 0, k == 15) for k in range(16)],
                    reads=("WSM", "SQ"), writes=("psw",))
                scopy(p, DQF[:, co], psw[0:64, :], reads=("psw",), writes=("DQF",))
                for tt in range(4):
                    mms(p, [(ps[tt][0:128, 0:4],
                             SQ[:, k * 512 + tt * 128:k * 512 + (tt + 1) * 128],
                             WSM[:, ofs["ww"] + k * 4: ofs["ww"] + (k + 1) * 4],
                             k == 0, k == 15) for k in range(16)],
                        reads=("WSM", "SQ"), writes=(f"ps{tt}",))
                    scopy(p, WGT[:, ((c - 2) * 4 + tt) * 4:((c - 2) * 4 + tt + 1) * 4],
                          ps[tt][:, 0:4], reads=(f"ps{tt}",), writes=("WGT",))
    p.barrier()

    def compress(CF, cfn, ZF, zfn, bio, nr):
        p.dma("sp", T0[0:nr, :], bio, reads=("T0",), writes=("T0",))
        vtt(p, T0[0:nr, :], ZF[0:nr, :], T0[0:nr, :], ALU.add,
            reads=(zfn, "T0"), writes=("T0",))
        act(p, T0[0:nr, :], T0[0:nr, :], AF.Exp, reads=("T0",), writes=("T0",))
        vreduce(p, T1b[0:nr, 0:64], T0[0:nr, :].rearrange("p (a b) -> p a b", b=32),
                ALU.add, reads=("T0",), writes=("T1b",))
        vtt(p, T0[0:nr, :], T0[0:nr, :], CF[0:nr, :], ALU.mult,
            reads=(cfn, "T0"), writes=("T0",))
        vreduce(p, T1b[0:nr, 64:128], T0[0:nr, :].rearrange("p (a b) -> p a b", b=32),
                ALU.add, reads=("T0",), writes=("T1b",))
        vrecip(p, T1b[0:nr, 0:64], T1b[0:nr, 0:64], reads=("T1b",), writes=("T1b",))
        vtt(p, T1b[0:nr, 64:128], T1b[0:nr, 64:128], T1b[0:nr, 0:64], ALU.mult,
            reads=("T1b",), writes=("T1b",))

    compress(CBF, "CBF", ZBF, "ZBF", io["biasbT"], 128)
    vcopy(p, T1b[:, 128:192], T1b[:, 64:128], reads=("T1b",), writes=("CKB",))
    compress(CIF, "CIF", ZIF, "ZIF", io["biasiT"], 64)
    vcopy(p, CKI[:], T1b[0:64, 64:128], reads=("T1b",), writes=("CKI",))
    act(p, T0[:, 0:64], T1b[:, 128:192], AF.Square, reads=("CKB",), writes=("T0",))
    mms(p, [(psl[0:1, 0:64], ONESCOL_F, T0[:, 0:64], True, True)],
        reads=("T0", "cst"), writes=("psl",))
    act(p, T0[0:1, 64:128], psl[0:1, 0:64], AF.Ln, scale=1.0 / 128,
        bias=cst[0:1, 3:4], reads=("psl", "cst"), writes=("T0",))
    act(p, T0[0:1, 64:128], T0[0:1, 64:128], AF.Exp, scale=-0.5,
        reads=("T0",), writes=("T0",))
    mms(p, [(psb[:, 0:64], orow[:], T0[0:1, 64:128], True, True)],
        reads=("T0", "orow"), writes=("psb",))
    vtt(p, T0[:, 0:64], T1b[:, 128:192], psb[:, 0:64], ALU.mult,
        reads=("CKB", "psb"), writes=("T0",))
    vts(p, KVN[:, 0:64], T0[:, 0:64], KNC, ALU.mult, reads=("T0", "masks"), writes=("KVN",))
    p.op("pe", lambda t, cb: cb(t.transpose(pst[0:64, 0:128], KVN[:, 0:64], idb[:])),
         reads=("KVN", "idb"), writes=("pst",))
    scopy(p, KVN[0:64, 64:192], pst[0:64, 0:128], reads=("pst",), writes=("KVNt",))
    for h in range(4):
        for cc in range(2):
            mms(p, [(psw[0:64, 0:512],
                     WSM[0:64, ofs["iuq"] + h * 64: ofs["iuq"] + (h + 1) * 64],
                     DQF[:, cc * 512:(cc + 1) * 512], True, True)],
                reads=("WSM", "DQF"), writes=("psw",))
            scopy(p, QIT[:, h * 1024 + cc * 512:h * 1024 + (cc + 1) * 512],
                  psw[0:64, 0:512], reads=("psw",), writes=("QIT",))
    for j in range(8):
        for h in range(4):
            mms(p, [(psw[0:128, 0:64], QIT[:, h * 1024 + j * 128:(h * 1024 + (j + 1) * 128)],
                     CKI[:], True, True)], reads=("QIT", "CKI"), writes=("psw",))
            act(p, T0[:, 0:64], psw[0:128, 0:64], AF.Relu, reads=("psw",), writes=("T0",))
            if h == 0:
                vts(p, ISC[:, j * 64:(j + 1) * 64], T0[:, 0:64],
                    WGT[:, j * 4 + h:j * 4 + h + 1], ALU.mult,
                    reads=("T0", "WGT"), writes=("ISC",))
            else:
                vstt(p, ISC[:, j * 64:(j + 1) * 64], T0[:, 0:64],
                     WGT[:, j * 4 + h:j * 4 + h + 1], ISC[:, j * 64:(j + 1) * 64],
                     ALU.mult, ALU.add, reads=("T0", "WGT", "ISC"), writes=("ISC",))
    for j in range(8):
        js = slice(j * 64, (j + 1) * 64)
        vtt(p, T0[:, 0:64], ISC[:, js], TB, ALU.add, reads=("ISC", "masks"), writes=("T0",))
        vts(p, T1b[:, 0:64], MV[:, js], -1.0, ALU.add, 1e30, ALU.mult,
            reads=("masks",), writes=("T1b",))
        vstt(p, T0[:, 0:64], T0[:, 0:64], 1.0, T1b[:, 0:64], ALU.mult, ALU.add,
             reads=("T1b", "T0"), writes=("T0",))
        vcopy(p, T0[:, 64:128], T0[:, 0:64], reads=("T0",), writes=("T0",))
        for r in range(32):
            vreduce(p, T1b[:, 64:65], T0[:, 64:128], ALU.max, reads=("T0",), writes=("T1b",))
            if r < 31:
                vts(p, T1b[:, 128:192], T0[:, 64:128], T1b[:, 64:65], ALU.is_ge,
                    reads=("T0", "T1b"), writes=("T1b",))
                vstt(p, T0[:, 64:128], T1b[:, 128:192], -1e30, T0[:, 64:128],
                     ALU.mult, ALU.add, reads=("T1b", "T0"), writes=("T0",))
        vts(p, T1b[:, 0:64], T0[:, 0:64], T1b[:, 64:65], ALU.is_ge,
            reads=("T0", "T1b"), writes=("T1b",))
        vtt(p, T1b[:, 0:64], T1b[:, 0:64], MV[:, js], ALU.mult,
            reads=("T1b", "masks"), writes=("T1b",))
        vts(p, SELA[:, js], T1b[:, 0:64], -1.0, ALU.add, 30000.0, ALU.mult,
            reads=("T1b",), writes=("SELA",))
    p.barrier()

    if io["dbg"]:
        p.dma("gd", io["dISC"], ISC[:], reads=("ISC",), writes=("dISC",))
        p.dma("gd", io["dSELA"], SELA[:], reads=("SELA",), writes=("dSELA",))
        p.dma("gd", io["dKVN"], KVN[:, 0:64], reads=("KVN", "KVNt"), writes=("dKVN",))
    phA = ExitStack()
    AT = phA.enter_context(nc.sbuf_tensor("ATb", [128, 16 * 1024], BF16))
    phQ = ExitStack()
    QT = phQ.enter_context(nc.sbuf_tensor("QTb", [128, 16 * 1024], BF16))
    with ExitStack() as ph:
        QCB = ph.enter_context(nc.sbuf_tensor("QCB", [128, 8 * 1024], BF16))
        p.dma("sp", QCB.ap().rearrange("p (a b) -> p a b", b=1024),
              io["qcscr"].rearrange("(a p) b -> p a b", p=128),
              reads=("qcscr",), writes=("QCB",))
        WQU = ph.enter_context(nc.sbuf_tensor("WQU", [128, 8 * 128], BF16))
        SQS = ph.enter_context(nc.sbuf_tensor("SQS", [128, 512], BF16))
        pslq = [ps[2], ps[3]]
        for m in range(16):
            p.dma("gd", WQU.ap().rearrange("p (a b) -> p a b", b=128),
                  io["w_qup"].rearrange("(a p) b -> p a b", p=128)[:, :, m * 128:(m + 1) * 128],
                  reads=("WQU",), writes=("WQU",))
            for cc in range(2):
                sl = slice(m * 1024 + cc * 512, m * 1024 + (cc + 1) * 512)
                csl = slice(cc * 512, (cc + 1) * 512)
                mms(p, [(psw[:], WQU[:, k * 128:(k + 1) * 128],
                         QCB[:, k * 1024 + cc * 512:k * 1024 + (cc + 1) * 512],
                         k == 0, k == 7) for k in range(8)],
                    reads=("WQU", "QCB"), writes=("psw",))
                scopy(p, QT[:, sl], psw[:], reads=("psw",), writes=("QT",))
                act(p, SQS[:], psw[:], AF.Square, reads=("psw",), writes=("SQS",))
                mms(p, [(psl[0:1, :], ocb[:], SQS[:], True, True)],
                    reads=("SQS", "ocb"), writes=("psl",))
                act(p, LBS[0:1, csl], psl[0:1, :], AF.Ln, scale=1.0 / 128,
                    bias=cst[0:1, 3:4], reads=("psl", "cst"), writes=("LBS",))
                act(p, LBS[0:1, csl], LBS[0:1, csl], AF.Exp, scale=-0.5,
                    reads=("LBS",), writes=("LBS",))
                mms(p, [(psb[:], orow[:], LBS[0:1, csl], True, True)],
                    reads=("LBS", "orow"), writes=("psb",))
                scopy(p, T1b[:, 1024:1536], psb[:], reads=("psb",), writes=("T1b",))
                mms(p, [(psw[:], prm[:], QT[:, sl], True, True)],
                    reads=("prm", "QT"), writes=("psw",))
                vtt(p, T0[:, 0:512], QT[:, sl], CQ[:, csl], ALU.mult,
                    reads=("QT", "ropec"), writes=("T0",))
                vtt(p, T0[:, 512:1024], psw[:], SQr[:, csl], ALU.mult,
                    reads=("psw", "ropec"), writes=("T0",))
                vtt(p, T0[:, 0:512], T0[:, 0:512], T0[:, 512:1024], ALU.add,
                    reads=("T0",), writes=("T0",))
                vtt(p, T0[:, 0:512], T0[:, 0:512], T1b[:, 1024:1536], ALU.mult,
                    reads=("T0", "T1b"), writes=("T0",))
                vts(p, QT[:, sl], T0[:, 0:512], QNC, ALU.mult,
                    reads=("T0", "masks"), writes=("QT",))
        for c in range(4):
            cs = slice(c * 512, (c + 1) * 512)
            act(p, T0[:, 0:512], SWKF[:, cs], AF.Square, reads=("SWKF",), writes=("T0",))
            vcopy(p, SQS[:], T0[:, 0:512], reads=("T0",), writes=("SQS",))
            mms(p, [(psl[0:1, :], ocb[:], SQS[:], True, True)],
                reads=("SQS", "ocb"), writes=("psl",))
            act(p, T0[0:1, 512:1024], psl[0:1, :], AF.Ln, scale=1.0 / 128,
                bias=cst[0:1, 3:4], reads=("psl", "cst"), writes=("T0",))
            act(p, T0[0:1, 512:1024], T0[0:1, 512:1024], AF.Exp, scale=-0.5,
                reads=("T0",), writes=("T0",))
            mms(p, [(psb[:], orow[:], T0[0:1, 512:1024], True, True)],
                reads=("T0", "orow"), writes=("psb",))
            vtt(p, T0[:, 0:512], SWKF[:, cs], psb[:], ALU.mult,
                reads=("SWKF", "psb"), writes=("T0",))
            vts(p, SWKR[:, cs], T0[:, 0:512], KNC, ALU.mult,
                reads=("T0", "masks"), writes=("SWKR",))
            mms(p, [(psw[:], prm[:], SWKR[:, cs], True, True)],
                reads=("prm", "SWKR"), writes=("psw",))
            vtt(p, T0[:, 0:512], SWKR[:, cs], CK[:, cs], ALU.mult,
                reads=("SWKR", "ropec"), writes=("T0",))
            vtt(p, T0[:, 512:1024], psw[:], SK[:, cs], ALU.mult,
                reads=("psw", "ropec"), writes=("T0",))
            vtt(p, SWKR[:, cs], T0[:, 0:512], T0[:, 512:1024], ALU.add,
                reads=("T0",), writes=("SWKR",))
    p.barrier()

    snke = phQ.enter_context(nc.sbuf_tensor("snke", [1, 16], BF16))
    snkr = phQ.enter_context(nc.sbuf_tensor("snkr", [1, 16], F32))
    p.dma("sp", snkr[:], io["sinkrow"], writes=("snkr",))
    act(p, snkr[:], snkr[:], AF.Exp, bias=snk[0:1, 1:2],
        reads=("snkr", "snk"), writes=("snkr",))
    vcopy(p, snke[:], snkr[:], reads=("snkr",), writes=("snke",))
    PPT = phQ.enter_context(nc.sbuf_tensor("PPT", [64, 128], BF16))
    PWB = phQ.enter_context(nc.sbuf_tensor("PWB2", [128, 64 + 5 * 128], BF16))
    RBSB = phQ.enter_context(nc.sbuf_tensor("RBSB", [128, 128], F32))
    for j in range(8):
        js = slice(j * 64, (j + 1) * 64)
        for h in range(16):
            qsl = QT[:, h * 1024 + j * 128:h * 1024 + (j + 1) * 128]
            mms(p, [(psw[0:128, 0:64], qsl, KVN[:, 0:64], True, True)],
                reads=("QT", "KVN"), writes=("psw",))
            vtt(p, T0[:, 0:64], psw[0:128, 0:64], SELA[:, js], ALU.add,
                reads=("psw", "SELA"), writes=("T0",))
            p.op("ve", lambda v, cb: cb(v.tensor_reduce(
                T1b[:, 0:1], T0[:, 0:64], AX.X, ALU.max, negate=True)),
                 reads=("T0",), writes=("T1b",))
            vts(p, T1b[:, 0:1], T1b[:, 0:1], RS, ALU.mult, reads=("T1b",), writes=("T1b",))
            act(p, T0[:, 64:128], T0[:, 0:64], AF.Exp, scale=RS, bias=T1b[:, 0:1],
                accum=T1b[:, 1:2], reads=("T0", "T1b"), writes=("T0", "T1b"))
            vrecip(p, T1b[:, 2:3], T1b[:, 1:2], reads=("T1b",), writes=("T1b",))
            vts(p, T0[:, 128:192], T0[:, 64:128], T1b[:, 2:3], ALU.mult,
                reads=("T0", "T1b"), writes=("T0",))
            vts(p, PWB[:, 0:64], T0[:, 128:192], HP[:, j:j + 1], ALU.mult,
                reads=("T0", "masks"), writes=("PWB",))
            p.op("pe", lambda t, cb: cb(t.transpose(pst[0:64, 0:128], PWB[:, 0:64], idb[:])),
                 reads=("PWB", "idb"), writes=("pst",))
            scopy(p, PPT[:], pst[0:64, 0:128], reads=("pst",), writes=("PPT",))
            mms(p, [(psw[0:128, 0:128], KVN[0:64, 64:192], PPT[:], True, True)],
                reads=("KVN", "PPT"), writes=("psw",))
            scopy(p, AT[:, h * 1024 + j * 128:h * 1024 + (j + 1) * 128], psw[0:128, 0:128],
                  reads=("psw",), writes=("AT",))
            for wi in range(5):
                kt = 4 + j + wi
                mms(p, [(ps[0][:, 0:128], SWKR[:, kt * 128:(kt + 1) * 128], qsl,
                         True, True)], reads=("SWKR", "QT"), writes=("ps0",))
                if wi == 0:
                    vtt(p, ps[0][:, 0:128], ps[0][:, 0:128], WTRI, ALU.add,
                        reads=("masks", "ps0"), writes=("ps0",))
                if wi == 4:
                    vtt(p, ps[0][:, 0:128], ps[0][:, 0:128], DTRI, ALU.add,
                        reads=("masks", "ps0"), writes=("ps0",))
                act(p, PWB[:, 64 + wi * 128:64 + (wi + 1) * 128],
                    ps[0][:, 0:128], AF.Exp, scale=RS,
                    bias=TMSW[:, j * 6 + wi:j * 6 + wi + 1],
                    reads=("ps0", "masks"), writes=("PWB",))
            mms(p, [(psl[0:1, 0:128], ocb[:],
                     PWB[:, 64 + wi * 128:64 + (wi + 1) * 128], wi == 0, wi == 4)
                    for wi in range(5)],
                reads=("PWB", "ocb"), writes=("psl",))
            if io["dbg"] and j == 0 and h == 0:
                scopy(p, T0[0:1, 1024:1152], psl[0:1, 0:128], reads=("psl",), writes=("T0",))
                p.dma("gd", io["dPSL"], T0[0:1, 1024:1152], reads=("T0",), writes=("dPSL",))
            vts(p, T1b[0:1, 0:128], psl[0:1, 0:128], snkr[0:1, h:h + 1], ALU.add,
                reads=("psl", "snkr"), writes=("T1b",))
            act(p, T1b[0:1, 0:128], T1b[0:1, 0:128], AF.Ln, reads=("T1b",), writes=("T1b",))
            act(p, T1b[0:1, 0:128], T1b[0:1, 0:128], AF.Exp, scale=-1.0,
                reads=("T1b",), writes=("T1b",))
            mms(p, [(psb[:, 0:128], orow[:], T1b[0:1, 0:128], True, True)],
                reads=("T1b", "orow"), writes=("psb",))
            scopy(p, RBSB[:], psb[:, 0:128], reads=("psb",), writes=("RBSB",))
            mms(p, [(ps[1][:, 0:128], SWV[:, (4 + j + wi) * 128:(4 + j + wi + 1) * 128],
                     PWB[:, 64 + wi * 128:64 + (wi + 1) * 128], wi == 0, wi == 4)
                    for wi in range(5)], reads=("PWB", "SWV"), writes=("ps1",))
            vtt(p, T0[:, 0:128], ps[1][:, 0:128], RBSB[:], ALU.mult,
                reads=("ps1", "RBSB"), writes=("T0",))
            if io["dbg"] and j == 0 and h == 0:
                p.dma("gd", io["dRB"], RBSB[:], reads=("RBSB",), writes=("dRB",))
                p.dma("gd", io["dSWC"], T0[:, 0:128], reads=("T0",), writes=("dSWC",))
                p.dma("gd", io["dLR"], T1b[0:1, 0:128], reads=("T1b",), writes=("dLR",))
            asl = AT[:, h * 1024 + j * 128:h * 1024 + (j + 1) * 128]
            vtt(p, asl, asl, T0[:, 0:128], ALU.add, reads=("T0", "AT"), writes=("AT",))
        if io["dbg"] and j == 0:
            p.dma("gd", io["dATs"].rearrange("p (a b) -> p a b", b=128),
                  AT.ap().rearrange("p (a b) -> p a b", b=1024)[:, :, 0:128],
                  reads=("AT",), writes=("dATs",))
    p.barrier()
    phQ.close()

    with ExitStack() as ph:
        GW = ph.enter_context(nc.sbuf_tensor("GW", [128, 4 * 128], BF16))
        FW = ph.enter_context(nc.sbuf_tensor("FW", [128, 16 * 128], BF16))
        OG = ph.enter_context(nc.sbuf_tensor("OG", [128, 16 * 1024], BF16))
        XRS = ph.enter_context(nc.sbuf_tensor("XRSa", [128, 512], F32))
        OU = ph.enter_context(nc.sbuf_tensor("OUa", [128, 512], F32))
        for g in range(4):
            for m in range(4):
                p.dma("gd", GW.ap().rearrange("p (a b) -> p a b", b=128),
                      io["group_w"][g].rearrange("(a p) b -> p a b", p=128)
                      [:, :, m * 128:(m + 1) * 128], reads=("GW",), writes=("GW",))
                for cc in range(2):
                    mms(p, [(psw[:], GW[:, k * 128:(k + 1) * 128],
                             AT[:, (g * 4 + k) * 1024 + cc * 512:(g * 4 + k) * 1024 + (cc + 1) * 512],
                             k == 0, k == 3) for k in range(4)],
                        reads=("GW", "AT"), writes=("psw",))
                    scopy(p, OG[:, (g * 4 + m) * 1024 + cc * 512:(g * 4 + m) * 1024 + (cc + 1) * 512],
                          psw[:], reads=("psw",), writes=("OG",))
        for m in range(16):
            p.dma("gd", FW.ap().rearrange("p (a b) -> p a b", b=128),
                  io["final_w"].rearrange("(a p) b -> p a b", p=128)[:, :, m * 128:(m + 1) * 128],
                  reads=("FW",), writes=("FW",))
            for cc in range(2):
                pi = cc % 2
                mms(p, [(ps[pi][:], FW[:, k * 128:(k + 1) * 128],
                         OG[:, k * 1024 + cc * 512:k * 1024 + (cc + 1) * 512],
                         k == 0, k == 15) for k in range(16)],
                    reads=("FW", "OG"), writes=(f"ps{pi}",))
                p.dma("sp", XRS[:], xr[:, m, 1024 + cc * 512:1024 + (cc + 1) * 512],
                      reads=("XRS",), writes=("XRS",))
                vstt(p, OU[:], ps[pi][:], 1.0, XRS[:], ALU.mult, ALU.add,
                     reads=(f"ps{pi}", "XRS"), writes=("OU",))
                p.dma("sp", io["p2"][m * 128:(m + 1) * 128, cc * 512:(cc + 1) * 512],
                      OU[:], reads=("OU",), writes=("p2",))
    phA.close()


def host_l1a_inputs(inputs, core, x1full):
    b, chunk = core // 2, core % 2
    th = 10000.0
    inv = 1.0 / th ** (np.arange(0, 64, 2, dtype=np.float64) / 64)
    own = np.arange(chunk * 1024, (chunk + 1) * 1024)
    srcpos = np.concatenate([own - 1024 + 2048 if chunk == 0 else own - 1024,
                             own]) if False else None
    other = np.arange((1 - chunk) * 1024, (2 - chunk) * 1024)
    src = np.concatenate([other, own])
    ang_q = own[:, None] * inv[None, :]
    ang_k = src[:, None] * inv[None, :]

    def ctab(ang, n):
        c = np.ones((128, n), np.float32)
        s = np.zeros((128, n), np.float32)
        c[0:32] = np.cos(ang).T.astype(np.float32)
        c[32:64] = np.cos(ang).T.astype(np.float32)
        s[0:32] = -np.sin(ang).T.astype(np.float32)
        s[32:64] = np.sin(ang).T.astype(np.float32)
        return c, s

    c128, s128 = ctab(ang_q, 1024)
    ck128, sk128 = ctab(ang_k, 2048)
    perm = np.zeros((128, 128), np.float32)
    for i in range(32):
        perm[i + 32, i] = 1.0
        perm[i, i + 32] = 1.0
    permT = perm.T

    loc2glob = (np.arange(64) + (32 if chunk == 0 else 0)) % 64
    gend = loc2glob * 32 + 31
    gt = own
    valid = (gend[None, :] < gt[:, None]).astype(np.float32)
    validmul = valid.reshape(8, 128, 64).transpose(1, 0, 2).reshape(128, 8 * 64)
    tbrow = np.tile((-1e-30 * loc2glob)[None, :], (128, 1)).astype(np.float32)
    hp = (gt >= 32).astype(np.float32).reshape(8, 128).T
    tmsw = np.full((128, 48), NEG, np.float32)
    for j in range(8):
        for wi in range(5):
            kt = 4 + j + wi
            gtile = kt if chunk == 1 else (kt + 8) % 16
            gq = (8 + j) if chunk == 1 else j
            ok = (gtile <= gq) and (gq - gtile <= 4)
            tmsw[:, j * 6 + wi] = -50.0 if ok else NEG - 50.0
    wintri = np.where(np.arange(128)[:, None] > np.arange(128)[None, :], 0.0, NEG)
    diagtri = np.where(np.arange(128)[:, None] <= np.arange(128)[None, :], 0.0, NEG)
    lsel16 = np.zeros((128, 16 * 128), np.float32)
    for h in range(16):
        lsel16[:, h * 128 + h] = 1.0
    oh16 = np.concatenate([np.tile((np.arange(16) == h).astype(np.float32)[:, None],
                                   (1, 128)) for h in range(16)], axis=1)
    im = dict(
        x1roll=_roll(x1full, chunk).T,
        w_qc=inputs["L1_attn_w_qc"], w_qup=inputs["L1_attn_w_qup"],
        w_kv_b=inputs["L1_attn_w_kv_b"], w_z_b=inputs["L1_attn_w_z_b"],
        w_k=inputs["L1_attn_w_k"], w_v=inputs["L1_attn_w_v"],
        wi_kv=inputs["L1_attn_wi_kv"], wi_z=inputs["L1_attn_wi_z"],
        w_dq=inputs["L1_attn_w_dq"], w_iuq=inputs["L1_attn_w_iuq"],
        w_w=inputs["L1_attn_w_w"],
        biasbT=np.tile(inputs["L1_attn_bias_b"].T, (1, 64)),
        biasiT=np.tile(inputs["L1_attn_bias_i"].T, (1, 64)),
        qnormcol=inputs["L1_attn_q_norm_w"].reshape(128, 1),
        knormcol=inputs["L1_attn_k_norm_w"].reshape(128, 1),
        sinkcol=inputs["L1_attn_sink_logit"].reshape(16, 1),
        group_w=inputs["L1_attn_group_w"], final_w=inputs["L1_attn_final_w"],
        c128=c128, s128=s128, ck128=ck128, sk128=sk128,
        permrope=permT, validmul=validmul, tbrow=tbrow, hpcol=hp,
        tmsw=tmsw, wintri=wintri, diagtri=diagtri, oh16=oh16,
        onescol=np.ones((128, 1), np.float32),
        onesrow=np.ones((1, 128), np.float32),
        epscol=np.full((128, 1), 1e-5, np.float32),
        eps6col=np.full((128, 1), 1e-6, np.float32),
        m50col=np.full((16, 1), -50.0, np.float32),
        sinkrow=inputs["L1_attn_sink_logit"].reshape(1, 16),
        q128eps=np.full((16, 1), 1e-6, np.float32),
    )
    im = {k: np.ascontiguousarray(v, dtype=np.float32) for k, v in im.items()}
    im["lsel16"] = lsel16.astype(ml_dtypes.bfloat16)
    im["identb"] = np.eye(128, dtype=np.float32).astype(ml_dtypes.bfloat16)
    return im


_BUILT = {}


def _get(name, builder):
    if name not in _BUILT:
        _BUILT[name] = builder()
    return _BUILT[name]


def kernel(**inputs):
    cores = list(range(8))
    nc = _get("l0a", build_l0a)
    ims = [host_l0a_inputs(inputs, c) for c in cores]
    r1 = run_bass_kernel_spmd(nc, ims, core_ids=cores).results
    nc = _get("l0f", build_l0f)
    ims = [host_l0f_inputs(inputs, c, np.asarray(r1[c]["xnf"]), np.asarray(r1[c]["p1"]))
           for c in cores]
    r2 = run_bass_kernel_spmd(nc, ims, core_ids=cores).results
    x1 = [np.asarray(r2[c]["x1"]) for c in cores]
    x1full = [np.concatenate([x1[2 * b].T, x1[2 * b + 1].T], axis=0) for b in range(4)]
    nc = _get("l1a", build_l1a)
    ims = [host_l1a_inputs(inputs, c, x1full[c // 2]) for c in cores]
    r3 = run_bass_kernel_spmd(nc, ims, core_ids=cores).results
    nc = _get("l1f", build_l1f)
    ims = [host_l1f_inputs(inputs, c, x1[c],
                           np.ascontiguousarray(np.asarray(r3[c]["p2"]).T))
           for c in cores]
    r4 = run_bass_kernel_spmd(nc, ims, core_ids=cores).results
    out = np.empty((4, 2048, 2048), np.float32)
    for c in cores:
        b, ch = c // 2, c % 2
        out[b, ch * 1024:(ch + 1) * 1024, :] = np.asarray(r4[c]["out"])
    return out
